# version 04
# baseline (speedup 1.0000x reference)
import numpy as np
import ml_dtypes
from contextlib import ExitStack
import concourse.bass as bass
import concourse.mybir as mybir
from concourse.bass_utils import run_bass_kernel_spmd

F32 = mybir.dt.float32
BF16 = mybir.dt.bfloat16
I32 = mybir.dt.int32
I8 = mybir.dt.int8
AF = mybir.ActivationFunctionType
ALU = mybir.AluOpType
AX = mybir.AxisListType

B, L, T, D = 2, 2048, 4096, 1024
NT, ND = T // 128, D // 128


class P:
    ENGS = ("sync", "scalar", "vector", "tensor", "gpsimd")

    def __init__(self, nc):
        self.nc = nc
        self.cum = 0
        self.q = {e: [] for e in self.ENGS}

    def _emit(self, eng, fn, inc):
        self.q[eng].append((self.cum, fn, inc))
        self.cum += inc

    def dma(self, dst, src, eng="sync"):
        self._emit(eng, lambda e, d=dst, s=src: e.dma_start(out=d, in_=s), 16)

    def mm(self, out, lhsT, rhs, start=True, stop=True):
        self._emit("tensor", lambda e, o=out, l=lhsT, r=rhs, st=start, sp=stop:
                   e.matmul(o, l, r, start=st, stop=sp, skip_group_check=True), 1)

    def trans(self, out, in_, ident):
        self._emit("tensor", lambda e, o=out, i=in_, d=ident: e.transpose(o, i, d), 1)

    def cp(self, out, in_, eng="vector"):
        self._emit(eng, lambda e, o=out, i=in_: e.tensor_copy(o, i), 1)

    def tt(self, out, a, b, op):
        self._emit("vector", lambda e, o=out, x=a, y=b, pp=op: e.tensor_tensor(o, x, y, pp), 1)

    def ts(self, out, a, s, op):
        self._emit("vector", lambda e, o=out, x=a, sc=s, pp=op:
                   e.tensor_scalar(o, x, sc, None, op0=pp), 1)

    def stt(self, out, in0, s, in1, op0, op1):
        self._emit("vector", lambda e, o=out, x=in0, sc=s, y=in1, p0=op0, p1=op1:
                   e.scalar_tensor_tensor(o, x, sc, y, p0, p1), 1)

    def rsum(self, out, in_, absv=None, neg=None):
        self._emit("vector", lambda e, o=out, i=in_, a=absv, n=neg:
                   e.reduce_sum(o, i, axis=AX.X, apply_absolute_value=a, negate=n), 1)

    def rmax(self, out, in_, absv=None, neg=None):
        self._emit("vector", lambda e, o=out, i=in_, a=absv, n=neg:
                   e.reduce_max(o, i, axis=AX.X, apply_absolute_value=a, negate=n), 1)

    def mset(self, ap, v):
        self._emit("vector", lambda e, a=ap, c=v: e.memset(a, c), 1)

    def recip(self, out, in_):
        self._emit("vector", lambda e, o=out, i=in_: e.reciprocal(o, i), 1)

    def act(self, out, in_, f, bias=None, scale=None, accum=None):
        def fn(e, o=out, i=in_, fu=f, b=bias, s=scale, a=accum):
            kw = {}
            if b is not None:
                kw["bias"] = b
            if s is not None:
                kw["scale"] = s
            if a is not None:
                kw["accum_out"] = a
            return e.activation(o, i, fu, **kw)
        self._emit("scalar", fn, 1)

    def coll_max(self, cin, cout):
        self._emit("gpsimd", lambda e, a=cin, b=cout: e.collective_compute(
            "AllReduce", ALU.max, replica_groups=[[0, 1, 2, 3, 4, 5, 6, 7]],
            ins=[a], outs=[b]), 1)


def build():
    nc = bass.Bass("TRN2", target_bir_lowering=False)
    p = P(nc)
    _uid = [0]

    def sbn(name, shape, dt=F32):
        _uid[0] += 1
        return nc.sbuf_tensor("%s_%d" % (name, _uid[0]), shape, dt)

    def din(name, shape, dt=F32):
        return nc.dram_tensor(name, shape, dt, kind="ExternalInput").ap()

    def dint(name, shape, dt=F32):
        return nc.dram_tensor(name, shape, dt, kind="Internal").ap()

    hs = din("hs", [T, D])
    a_wq = din("a_wq", [D, D]); a_wk = din("a_wk", [D, D]); a_wv = din("a_wv", [D, D])
    a_wb = din("a_wb", [D, 4])
    a_cq = din("a_cq", [D, 4]); a_ck = din("a_ck", [D, 4]); a_cv = din("a_cv", [D, 4])
    a_fs = din("a_fs", [D, 3]); a_fl = din("a_fl", [D, 63])
    a_w1 = din("a_w1", [1056, 2048]); a_b1 = din("a_b1", [2048, 1])
    a_w2 = din("a_w2", [2048, 16]); a_b2 = din("a_b2", [1, 16])
    a_glt = din("a_glt", [1, 4]); a_onw = din("a_onw", [256, 1])
    a_wo = din("a_wo", [D, D])
    b_wq = din("b_wq", [D, D]); b_wk = din("b_wk", [D, D]); b_wv = din("b_wv", [D, D])
    b_bq = din("b_bq", [1, D]); b_bk = din("b_bk", [1, D]); b_bv = din("b_bv", [1, D])
    f_w8 = din("f_w8", [D, D], I8)
    f_ws = din("f_ws", [1, D]); f_bi = din("f_bi", [1, D])
    c_qp = din("c_qp", [D, D]); c_kp = din("c_kp", [D, 64]); c_vp = din("c_vp", [D, 64])
    c_op = din("c_op", [D, D])
    d_wq = din("d_wq", [D, D]); d_wk = din("d_wk", [D, D]); d_wv = din("d_wv", [D, D])
    d_wo = din("d_wo", [D, D])
    d_bq = din("d_bq", [1, D]); d_bk = din("d_bk", [1, D]); d_bv = din("d_bv", [1, D])
    d_bo = din("d_bo", [1, D])
    d_l1w = din("d_l1w", [1, D]); d_l1b = din("d_l1b", [1, D])
    d_l2w = din("d_l2w", [1, D]); d_l2b = din("d_l2b", [1, D])
    m_gw = din("m_gw", [D, 32]); m_cb = din("m_cb", [1, 32])
    m_gu = din("m_gu", [32 * D, D])
    m_dn = din("m_dn", [32 * 512, D])
    m_sg = din("m_sg", [D, D]); m_su = din("m_su", [D, D]); m_sd = din("m_sd", [D, D])
    c_eye = din("c_eye", [128, 128])
    c_ones = din("c_ones", [1, 512])
    c_mls = din("c_mls", [128, 128])
    c_mnus = din("c_mnus", [128, 128])
    c_mui = din("c_mui", [128, 128])

    out_d = nc.dram_tensor("out", [T, D], F32, kind="ExternalOutput").ap()

    x_d = dint("x_d", [T, D])
    y_d = dint("y_d", [T, D])
    qc_d = dint("qc_d", [2 * D, L], BF16)
    kc_d = dint("kc_d", [2 * D, L], BF16)
    vc_d = dint("vc_d", [2 * D, L], BF16)
    fs_d = dint("fs_d", [T, D], BF16)
    fl_d = dint("fl_d", [T, D], BF16)
    vb_d = dint("vb_d", [T, D], BF16)
    do_d = dint("do_d", [T, D], BF16)
    wq_s = dint("wq_s", [D, D], BF16)
    wk_s = dint("wk_s", [D, D], BF16)
    wv_s = dint("wv_s", [D, D], BF16)
    cc_i = [dint("cc_i%d" % i, [1, 1]) for i in range(2)]
    cc_o = [dint("cc_o%d" % i, [1, 1]) for i in range(2)]

    es = ExitStack()
    E = es.enter_context

    yt = E(sbn("yt", [128, ND * T], BF16))
    idb = E(sbn("idb", [128, 128], BF16))
    onf = E(sbn("onf", [1, 512], F32))
    onb = E(sbn("onb", [1, 512], BF16))
    onc = E(sbn("onc", [128, 1], BF16))
    mls = E(sbn("mls", [128, 128], F32))
    mnus = E(sbn("mnus", [128, 128], F32))
    mui = E(sbn("mui", [128, 128], F32))
    mlsb = E(sbn("mlsb", [128, 128], BF16))
    muib = E(sbn("muib", [128, 128], BF16))
    f1 = E(sbn("f1", [128, 1024], F32))
    f2 = E(sbn("f2", [128, 1024], F32))
    sq = E(sbn("sq", [128, 1024], F32))
    sqb = E(sbn("sqb", [128, 1024], BF16))
    comb = E(sbn("comb", [128, 1024], F32))
    sml = E(sbn("sml", [128, 64], F32))
    ob64 = E(sbn("ob64", [64, 512], BF16))

    pA = E(nc.psum_tensor("pA", [128, 512], F32))
    pB = E(nc.psum_tensor("pB", [128, 512], F32))
    pC = E(nc.psum_tensor("pC", [128, 512], F32))
    pD = E(nc.psum_tensor("pD", [128, 512], F32))
    pE = E(nc.psum_tensor("pE", [128, 512], F32))
    pT0 = E(nc.psum_tensor("pT0", [128, 128], BF16))
    pT1 = E(nc.psum_tensor("pT1", [128, 128], BF16))

    p.dma(f1[:, 0:128], c_eye)
    p.cp(idb[:, :], f1[:, 0:128])
    p.dma(f1[0:1, 0:512], c_ones)
    p.cp(onf[:, :], f1[0:1, 0:512])
    p.cp(onb[:, :], f1[0:1, 0:512])
    p.mset(onc[:, :], 1.0)
    p.dma(mls[:, :], c_mls)
    p.dma(mnus[:, :], c_mnus)
    p.dma(mui[:, :], c_mui)
    p.cp(mlsb[:, :], mls[:, :])
    p.cp(muib[:, :], mui[:, :])

    def rsqrt_(dst, src, bias, scale):
        p.ts(sml[:, 60:61], src, scale, ALU.mult)
        p.ts(sml[:, 60:61], sml[:, 60:61], bias, ALU.add)
        p.recip(sml[:, 61:62], sml[:, 60:61])
        p.act(dst, sml[:, 61:62], AF.Sqrt)

    def ln_tile(src_ap, dst_f32):
        p.rsum(sml[:, 0:1], src_ap)
        p.ts(sml[:, 0:1], sml[:, 0:1], 1.0 / 1024.0, ALU.mult)
        p.ts(dst_f32, src_ap, sml[:, 0:1], ALU.subtract)
        p.act(sq[:, 0:1024], dst_f32, AF.Square, accum=sml[:, 1:2])
        rsqrt_(sml[:, 2:3], sml[:, 1:2], 1e-5, 1.0 / 1024.0)
        p.ts(dst_f32, dst_f32, sml[:, 2:3], ALU.mult)

    def to_yt(tile_f32, t):
        p.cp(sqb[:, :], tile_f32)
        for k in range(ND):
            ps = pT0 if k % 2 == 0 else pT1
            p.trans(ps[:, 0:128], sqb[:, 128 * k:128 * k + 128], idb[:, :])
            p.cp(yt[:, 4096 * k + 128 * t: 4096 * k + 128 * t + 128], ps[:, 0:128])

    def emit_ln_yt(xsrc, store_y=False, double_ln=None):
        for t in range(NT):
            p.dma(f1[:, :], xsrc[128 * t:128 * t + 128, :])
            ln_tile(f1[:, :], f2[:, :])
            if store_y:
                p.dma(y_d[128 * t:128 * t + 128, :], f2[:, :], eng="gpsimd")
            if double_ln is not None:
                w_b, b_b = double_ln
                p.cp(f1[:, :], f2[:, :])
                ln_tile(f1[:, :], f2[:, :])
                p.tt(f2[:, :], f2[:, :], w_b, ALU.mult)
                p.tt(f2[:, :], f2[:, :], b_b, ALU.add)
            to_yt(f2[:, :], t)

    def bcast_row_f32(row_ap, dst_sb, n):
        for c in range(0, n, 512):
            w = min(512, n - c)
            p.mm(pA[:, 0:w], onf[0:1, 0:128], row_ap[0:1, c:c + w])
            p.cp(dst_sb[:, c:c + w], pA[:, 0:w])

    def resid_add(xsrc, xdst, t, add_f32):
        p.dma(f1[:, :], xsrc[128 * t:128 * t + 128, :])
        p.tt(f1[:, :], f1[:, :], add_f32, ALU.add)
        p.dma(xdst[128 * t:128 * t + 128, :], f1[:, :], eng="gpsimd")

    def sub_aa(xsrc, xdst):
        emit_ln_yt(xsrc)
        with ExitStack() as s1:
            beta = s1.enter_context(sbn("beta", [128, 128], F32))
            wb = s1.enter_context(sbn("wbb", [128, 32], BF16))
            for k in range(ND):
                p.dma(f2[:, 0:4], a_wb[128 * k:128 * k + 128, :])
                p.cp(wb[:, 4 * k:4 * k + 4], f2[:, 0:4])
            for t in range(NT):
                for k in range(ND):
                    p.mm(pA[:, 0:4], yt[:, 4096 * k + 128 * t: 4096 * k + 128 * t + 128],
                         wb[:, 4 * k:4 * k + 4], start=(k == 0), stop=(k == ND - 1))
                p.act(beta[:, 4 * t:4 * t + 4], pA[:, 0:4], AF.Sigmoid)
            for wsrc, csrc, dst in ((a_wq, a_cq, qc_d), (a_wk, a_ck, kc_d), (a_wv, a_cv, vc_d)):
                with ExitStack() as s2:
                    wbf = s2.enter_context(sbn("wbf", [128, ND * 1024], BF16))
                    cw = s2.enter_context(sbn("cw", [128, 32], F32))
                    cb = s2.enter_context(sbn("cb", [128, 2051], F32))
                    acc = s2.enter_context(sbn("acc", [128, 2048], F32))
                    for k in range(ND):
                        p.dma(f2[:, :], wsrc[128 * k:128 * k + 128, :])
                        p.cp(wbf[:, 1024 * k:1024 * k + 1024], f2[:, :])
                        p.dma(cw[:, 4 * k:4 * k + 4], csrc[128 * k:128 * k + 128, :])
                    for b in range(B):
                        for kt in range(ND):
                            p.mset(cb[:, 0:3], 0.0)
                            for c4 in range(4):
                                for k in range(ND):
                                    p.mm(pA[:, 0:512],
                                         wbf[:, 1024 * k + 128 * kt:1024 * k + 128 * kt + 128],
                                         yt[:, 4096 * k + 2048 * b + 512 * c4:
                                            4096 * k + 2048 * b + 512 * c4 + 512],
                                         start=(k == 0), stop=(k == ND - 1))
                                p.cp(cb[:, 3 + 512 * c4:3 + 512 * c4 + 512], pA[:, 0:512])
                            for j in range(4):
                                if j == 0:
                                    p.ts(acc[:, :], cb[:, 0:2048], cw[:, 4 * kt:4 * kt + 1], ALU.mult)
                                else:
                                    p.stt(acc[:, :], cb[:, j:j + 2048], cw[:, 4 * kt + j:4 * kt + j + 1],
                                          acc[:, :], ALU.mult, ALU.add)
                            p.act(acc[:, :], acc[:, :], AF.Silu)
                            p.cp(sqb[:, 0:1024], acc[:, 0:1024])
                            p.cp(sqb[0:128, 0:1024] if False else sqb[:, 0:1024], acc[:, 0:1024]) if False else None
                            p.dma(dst[1024 * b + 128 * kt:1024 * b + 128 * kt + 128, 0:1024],
                                  sqb[:, 0:1024], eng="gpsimd")
                            p.cp(sqb[:, 0:1024], acc[:, 1024:2048])
                            p.dma(dst[1024 * b + 128 * kt:1024 * b + 128 * kt + 128, 1024:2048],
                                  sqb[:, 0:1024], eng="gpsimd")
            with ExitStack() as s2:
                fw = s2.enter_context(sbn("fw", [128, 8 * 66], F32))
                fb = s2.enter_context(sbn("fb", [128, 2110], BF16))
                accl = s2.enter_context(sbn("accl", [128, 2048], F32))
                for k in range(ND):
                    p.dma(fw[:, 66 * k:66 * k + 3], a_fs[128 * k:128 * k + 128, :])
                    p.dma(fw[:, 66 * k + 3:66 * k + 66], a_fl[128 * k:128 * k + 128, :])
                for b in range(B):
                    for kt in range(ND):
                        p.mset(fb[:, 0:62], 0.0)
                        p.dma(fb[:, 62:2110], vc_d[1024 * b + 128 * kt:1024 * b + 128 * kt + 128, :])
                        for (dst, off0, ntap, woff) in ((fs_d, 60, 3, 66 * kt),
                                                        (fl_d, 0, 63, 66 * kt + 3)):
                            for j in range(ntap):
                                if j == 0:
                                    p.ts(accl[:, :], fb[:, off0:off0 + 2048],
                                         fw[:, woff:woff + 1], ALU.mult)
                                else:
                                    p.stt(accl[:, :], fb[:, off0 + j:off0 + j + 2048],
                                          fw[:, woff + j:woff + j + 1],
                                          accl[:, :], ALU.mult, ALU.add)
                            p.cp(sqb[:, 0:1024] if False else sqb[:, 0:1024], accl[:, 0:1024]) if False else None
                            for half in range(2):
                                p.cp(sqb[:, 0:1024], accl[:, 1024 * half:1024 * half + 1024])
                                for tt_ in range(8):
                                    tg = 8 * half + tt_
                                    ps = pT0 if tt_ % 2 == 0 else pT1
                                    p.trans(ps[:, 0:128], sqb[:, 128 * tt_:128 * tt_ + 128], idb[:, :])
                                    p.cp(sqb[0:0, 0:0] if False else f2[:, 0:128], ps[:, 0:128]) if False else None
                                    p.cp(beta[0:0, 0:0] if False else sml[:, 32:33], onc[:, :]) if False else None
                                    p.cp(f2[:, 0:128], ps[:, 0:128])
                                    p.cp(sml[:, 32:33], sml[:, 32:33]) if False else None
                                    p.cp(sqb2_unused := None, None) if False else None
                                    p.cp(f1[:, 0:128], f2[:, 0:128]) if False else None
                                    p.dma(dst[2048 * b + 128 * tg:2048 * b + 128 * tg + 128,
                                              128 * kt:128 * kt + 128], f2[:, 0:128], eng="gpsimd")
                        for half in range(2):
                            for tt_ in range(8):
                                tg = 8 * half + tt_
                                ps = pT0 if tt_ % 2 == 0 else pT1
                                p.trans(ps[:, 0:128],
                                        fb[:, 62 + 1024 * half + 128 * tt_:62 + 1024 * half + 128 * tt_ + 128],
                                        idb[:, :])
                                p.cp(f2[:, 0:128], ps[:, 0:128])
                                p.dma(vb_d[2048 * b + 128 * tg:2048 * b + 128 * tg + 128,
                                           128 * kt:128 * kt + 128], f2[:, 0:128], eng="gpsimd")
            for b in range(B):
                for h in range(4):
                    with ExitStack() as s2:
                        qct = s2.enter_context(sbn("qct", [128, 4096], BF16))
                        kct = s2.enter_context(sbn("kct", [128, 4096], BF16))
                        vct = s2.enter_context(sbn("vct", [128, 4096], BF16))
                        qtk = s2.enter_context(sbn("qtk", [128, 4096], BF16))
                        ktk = s2.enter_context(sbn("ktk", [128, 4096], BF16))
                        vtk = s2.enter_context(sbn("vtk", [128, 4096], BF16))
                        kbk = s2.enter_context(sbn("kbk", [128, 4096], BF16))
                        uf = s2.enter_context(sbn("uf", [128, 512], F32))
                        ub = s2.enter_context(sbn("ub", [128, 512], BF16))
                        ptb = s2.enter_context(sbn("ptb", [128, 128], BF16))
                        pb_ = s2.enter_context(sbn("pb_", [128, 128], BF16))
                        atb = s2.enter_context(sbn("atb", [128, 128], BF16))
                        wtb = s2.enter_context(sbn("wtb", [128, 256], BF16))
                        sbf = s2.enter_context(sbn("sbf", [128, 512], BF16))
                        u2b = s2.enter_context(sbn("u2b", [128, 256], BF16))
                        for dsub in range(2):
                            r0 = 1024 * b + 256 * h + 128 * dsub
                            p.dma(qct[:, 2048 * dsub:2048 * dsub + 2048], qc_d[r0:r0 + 128, :])
                            p.dma(kct[:, 2048 * dsub:2048 * dsub + 2048], kc_d[r0:r0 + 128, :])
                            p.dma(vct[:, 2048 * dsub:2048 * dsub + 2048], vc_d[r0:r0 + 128, :])
                        for src, dst in ((qct, qtk), (kct, ktk), (vct, vtk)):
                            for dsub in range(2):
                                for tt_ in range(16):
                                    ps = pT0 if tt_ % 2 == 0 else pT1
                                    p.trans(ps[:, 0:128],
                                            src[:, 2048 * dsub + 128 * tt_:2048 * dsub + 128 * tt_ + 128],
                                            idb[:, :])
                                    p.cp(dst[:, 256 * tt_ + 128 * dsub:256 * tt_ + 128 * dsub + 128],
                                         ps[:, 0:128])
                        for tt_ in range(16):
                            bcol = beta[:, 4 * (16 * b + tt_) + h:4 * (16 * b + tt_) + h + 1]
                            for buf in (qtk, ktk):
                                p.act(sq[:, 0:256], buf[:, 256 * tt_:256 * tt_ + 256],
                                      AF.Square, accum=sml[:, 3:4])
                                rsqrt_(sml[:, 4:5], sml[:, 3:4], 1e-6, 1.0)
                                p.ts(buf[:, 256 * tt_:256 * tt_ + 256],
                                     buf[:, 256 * tt_:256 * tt_ + 256], sml[:, 4:5], ALU.mult)
                            p.ts(kbk[:, 256 * tt_:256 * tt_ + 256],
                                 ktk[:, 256 * tt_:256 * tt_ + 256], bcol, ALU.mult)
                            p.ts(vtk[:, 256 * tt_:256 * tt_ + 256],
                                 vtk[:, 256 * tt_:256 * tt_ + 256], bcol, ALU.mult)
                        for src, dst in ((qtk, qct), (ktk, kct)):
                            for tt_ in range(16):
                                for dsub in range(2):
                                    ps = pT0 if dsub == 0 else pT1
                                    p.trans(ps[:, 0:128],
                                            src[:, 256 * tt_ + 128 * dsub:256 * tt_ + 128 * dsub + 128],
                                            idb[:, :])
                                    p.cp(dst[:, 2048 * dsub + 128 * tt_:2048 * dsub + 128 * tt_ + 128],
                                         ps[:, 0:128])
                        for g in range(16):
                            gc = 128 * g
                            for dsub in range(2):
                                ps = pT0 if dsub == 0 else pT1
                                p.trans(ps[:, 0:128],
                                        kbk[:, 256 * g + 128 * dsub:256 * g + 128 * dsub + 128],
                                        idb[:, :])
                                p.cp(wtb[:, 128 * dsub:128 * dsub + 128], ps[:, 0:128])
                            for dsub in range(2):
                                p.mm(pC[:, 0:128], kct[:, 2048 * dsub + gc:2048 * dsub + gc + 128],
                                     wtb[:, 128 * dsub:128 * dsub + 128],
                                     start=(dsub == 0), stop=(dsub == 1))
                            p.tt(sq[:, 0:128], pC[:, 0:128], mnus[:, :], ALU.mult)
                            p.cp(ptb[:, :], sq[:, 0:128])
                            p.trans(pT0[:, 0:128], ptb[:, :], idb[:, :])
                            p.cp(pb_[:, :], pT0[:, 0:128])
                            p.cp(uf[:, 0:256], vtk[:, 256 * g:256 * g + 256])
                            p.cp(uf[:, 256:512], kbk[:, 256 * g:256 * g + 256])
                            p.cp(ub[:, :], uf[:, :])
                            for lvl in range(7):
                                p.mm(pC[:, 0:512], ptb[:, :], ub[:, :])
                                p.tt(uf[:, :], uf[:, :], pC[:, 0:512], ALU.add)
                                p.cp(ub[:, :], uf[:, :])
                                if lvl < 6:
                                    p.mm(pC[:, 0:128], ptb[:, :], pb_[:, :])
                                    p.cp(pb_[:, :], pC[:, 0:128])
                                    p.trans(pT0[:, 0:128], pb_[:, :], idb[:, :])
                                    p.cp(ptb[:, :], pT0[:, 0:128])
                            for dsub in range(2):
                                ps = pT0 if dsub == 0 else pT1
                                p.trans(ps[:, 0:128], ub[:, 256 + 128 * dsub:256 + 128 * dsub + 128],
                                        idb[:, :])
                                p.cp(wtb[:, 128 * dsub:128 * dsub + 128], ps[:, 0:128])
                            if g > 0:
                                p.cp(sbf[:, :], pD[:, :])
                                for dsub in range(2):
                                    p.mm(pC[:, 0:256], wtb[:, 128 * dsub:128 * dsub + 128],
                                         sbf[:, 256 * dsub:256 * dsub + 256],
                                         start=(dsub == 0), stop=(dsub == 1))
                                p.tt(uf[:, 0:256], uf[:, 0:256], pC[:, 0:256], ALU.subtract)
                            p.cp(u2b[:, :], uf[:, 0:256])
                            for dsub in range(2):
                                p.mm(pC[:, 0:128], kct[:, 2048 * dsub + gc:2048 * dsub + gc + 128],
                                     qct[:, 2048 * dsub + gc:2048 * dsub + gc + 128],
                                     start=(dsub == 0), stop=(dsub == 1))
                            p.tt(sq[:, 0:128], pC[:, 0:128], mui[:, :], ALU.mult)
                            p.cp(atb[:, :], sq[:, 0:128])
                            if g > 0:
                                for dsub in range(2):
                                    p.mm(pE[:, 0:256], qct[:, 2048 * dsub + gc:2048 * dsub + gc + 128],
                                         sbf[:, 256 * dsub:256 * dsub + 256],
                                         start=(dsub == 0), stop=False)
                                p.mm(pE[:, 0:256], atb[:, :], u2b[:, :], start=False, stop=True)
                            else:
                                p.mm(pE[:, 0:256], atb[:, :], u2b[:, :], start=True, stop=True)
                            for dsub in range(2):
                                p.mm(pD[:, 256 * dsub:256 * dsub + 256],
                                     ktk[:, 256 * g + 128 * dsub:256 * g + 128 * dsub + 128],
                                     u2b[:, :], start=(g == 0), stop=(g == 15))
                            p.cp(sqb[:, 0:256], pE[:, 0:256])
                            trow = 2048 * b + gc
                            p.dma(do_d[trow:trow + 128, 256 * h:256 * h + 256],
                                  sqb[:, 0:256], eng="gpsimd")
            with ExitStack() as s2:
                w1b = s2.enter_context(sbn("w1b", [128, ND * 2048], BF16))
                w1s = s2.enter_context(sbn("w1s", [32, 2048], BF16))
                w2b = s2.enter_context(sbn("w2b", [128, 16 * 16], BF16))
                b2b = s2.enter_context(sbn("b2b", [1, 16], BF16))
                b1s = s2.enter_context(sbn("b1s", [128, 16], F32))
                brs = s2.enter_context(sbn("brs", [128, 4 * 1024], BF16))
                stt_ = s2.enter_context(sbn("stt_", [128, 128], BF16))
                stT = s2.enter_context(sbn("stT", [32, 4096], BF16))
                hT = s2.enter_context(sbn("hT", [128, 16 * 512], BF16))
                lgt = s2.enter_context(sbn("lgt", [128, 16], F32))
                prb = s2.enter_context(sbn("prb", [128, 16], F32))
                tmpr = s2.enter_context(sbn("tmpr", [1, 16], F32))
                tpb = s2.enter_context(sbn("tpb", [128, 16], F32))
                ot = s2.enter_context(sbn("ot", [128, 1024], F32))
                oTt = s2.enter_context(sbn("oTt", [128, 1024], BF16))
                wob = s2.enter_context(sbn("wob", [128, ND * 1024], BF16))
                onw = s2.enter_context(sbn("onw", [128, 2], F32))
                for k in range(ND):
                    p.dma(f2[:, :], a_w1[128 * k:128 * k + 128, 0:1024])
                    p.cp(w1b[:, 2048 * k:2048 * k + 1024], f2[:, :])
                    p.dma(f2[:, :], a_w1[128 * k:128 * k + 128, 1024:2048])
                    p.cp(w1b[:, 2048 * k + 1024:2048 * k + 2048], f2[:, :])
                p.dma(f2[0:32, 0:1024], a_w1[1024:1056, 0:1024])
                p.cp(w1s[:, 0:1024], f2[0:32, 0:1024])
                p.dma(f2[0:32, 0:1024], a_w1[1024:1056, 1024:2048])
                p.cp(w1s[:, 1024:2048], f2[0:32, 0:1024])
                for k in range(16):
                    p.dma(f2[:, 0:16], a_w2[128 * k:128 * k + 128, :])
                    p.cp(w2b[:, 16 * k:16 * k + 16], f2[:, 0:16])
                p.dma(f2[0:1, 0:16], a_b2)
                p.cp(b2b[:, :], f2[0:1, 0:16])
                p.dma(onw[:, 0:1], a_onw[0:128, :])
                p.dma(onw[:, 1:2], a_onw[128:256, :])
                for k in range(ND):
                    p.dma(f2[:, :], a_wo[128 * k:128 * k + 128, :])
                    p.ts(f2[:, :], f2[:, :], onw[:, (k % 2):(k % 2) + 1], ALU.mult)
                    p.cp(wob[:, 1024 * k:1024 * k + 1024], f2[:, :])
                p.dma(f2[0:1, 0:4], a_glt)
                p.act(f2[0:1, 12:16], f2[0:1, 0:4], AF.Abs)
                p.act(f2[0:1, 16:20], f2[0:1, 12:16], AF.Exp, scale=-1.0)
                p.ts(f2[0:1, 16:20], f2[0:1, 16:20], 1.0, ALU.add)
                p.act(f2[0:1, 20:24], f2[0:1, 16:20], AF.Ln)
                p.ts(f2[0:1, 4:8], f2[0:1, 0:4], 0.0, ALU.max)
                p.tt(f2[0:1, 4:8], f2[0:1, 4:8], f2[0:1, 20:24], ALU.add)
                p.ts(f2[0:1, 4:8], f2[0:1, 4:8], 1e-4, ALU.add)
                p.recip(f2[0:1, 8:12], f2[0:1, 4:8])
                for hh in range(4):
                    for jj in range(4):
                        p.cp(tmpr[0:1, 4 * hh + jj:4 * hh + jj + 1], f2[0:1, 8 + hh:8 + hh + 1])
                bcast_row_f32(tmpr, tpb, 16)
                for k in range(16):
                    p.dma(b1s[:, k:k + 1], a_b1[128 * k:128 * k + 128, :])
                for t in range(NT):
                    tr = 128 * t
                    for i, src in enumerate((fs_d, fl_d, do_d, vb_d)):
                        p.dma(brs[:, 1024 * i:1024 * i + 1024], src[tr:tr + 128, :])
                    for i in range(4):
                        for hh in range(4):
                            seg = brs[:, 1024 * i + 256 * hh:1024 * i + 256 * hh + 256]
                            p.rsum(sml[:, 8:9], seg)
                            p.ts(sml[:, 8:9], sml[:, 8:9], 1.0 / 256.0, ALU.mult)
                            p.act(sq[:, 0:256], seg, AF.Square, accum=sml[:, 9:10])
                            p.ts(sml[:, 9:10], sml[:, 9:10], 1.0 / 256.0, ALU.mult)
                            p.act(sml[:, 11:12], sml[:, 8:9], AF.Square)
                            p.tt(sml[:, 9:10], sml[:, 9:10], sml[:, 11:12], ALU.subtract)
                            p.ts(sml[:, 9:10], sml[:, 9:10], 1e-6, ALU.max)
                            p.act(sml[:, 10:11], sml[:, 9:10], AF.Sqrt)
                            p.cp(stt_[:, 8 * i + hh:8 * i + hh + 1], sml[:, 8:9])
                            p.cp(stt_[:, 8 * i + 4 + hh:8 * i + 4 + hh + 1], sml[:, 10:11])
                    p.mset(stt_[:, 32:128], 0.0)
                    p.trans(pT0[:, 0:128], stt_[:, :], idb[:, :])
                    p.cp(stT[:, 128 * t:128 * t + 128], pT0[0:32, 0:128])
                for ch in range(8):
                    for m in range(16):
                        for k in range(ND):
                            p.mm(pA[:, 0:512], w1b[:, 2048 * k + 128 * m:2048 * k + 128 * m + 128],
                                 yt[:, 4096 * k + 512 * ch:4096 * k + 512 * ch + 512],
                                 start=(k == 0), stop=False)
                        p.mm(pA[:, 0:512], w1s[:, 128 * m:128 * m + 128],
                             stT[:, 512 * ch:512 * ch + 512], start=False, stop=True)
                        p.act(hT[:, 512 * m:512 * m + 512], pA[:, 0:512], AF.Gelu,
                              bias=b1s[:, m:m + 1], scale=1.0)
                    for tl in range(4):
                        t = 4 * ch + tl
                        for m in range(16):
                            p.mm(pB[:, 0:16], hT[:, 512 * m + 128 * tl:512 * m + 128 * tl + 128],
                                 w2b[:, 16 * m:16 * m + 16], start=(m == 0), stop=False)
                        p.mm(pB[:, 0:16], onb[0:1, 0:128], b2b[:, :], start=False, stop=True)
                        p.cp(lgt[:, :], pB[:, 0:16])
                        p.tt(lgt[:, :], lgt[:, :], tpb[:, :], ALU.mult)
                        for hh in range(4):
                            sl = lgt[:, 4 * hh:4 * hh + 4]
                            p.rmax(sml[:, 12:13], sl)
                            p.ts(sml[:, 12:13], sml[:, 12:13], -1.0, ALU.mult)
                            p.act(prb[:, 4 * hh:4 * hh + 4], sl, AF.Exp,
                                  bias=sml[:, 12:13], scale=1.0, accum=sml[:, 13:14])
                            p.recip(sml[:, 14:15], sml[:, 13:14])
                            p.ts(prb[:, 4 * hh:4 * hh + 4], prb[:, 4 * hh:4 * hh + 4],
                                 sml[:, 14:15], ALU.mult)
                        tr = 128 * t
                        for i, src in enumerate((fs_d, fl_d, do_d, vb_d)):
                            p.dma(brs[:, 1024 * i:1024 * i + 1024], src[tr:tr + 128, :])
                        for hh in range(4):
                            for jj in range(4):
                                seg = brs[:, 1024 * jj + 256 * hh:1024 * jj + 256 * hh + 256]
                                wc = prb[:, 4 * hh + jj:4 * hh + jj + 1]
                                if jj == 0:
                                    p.ts(ot[:, 256 * hh:256 * hh + 256], seg, wc, ALU.mult)
                                else:
                                    p.stt(ot[:, 256 * hh:256 * hh + 256], seg, wc,
                                          ot[:, 256 * hh:256 * hh + 256], ALU.mult, ALU.add)
                            p.act(sq[:, 0:256], ot[:, 256 * hh:256 * hh + 256], AF.Square,
                                  accum=sml[:, 15:16])
                            rsqrt_(sml[:, 16:17], sml[:, 15:16], 1e-5, 1.0 / 256.0)
                            p.ts(ot[:, 256 * hh:256 * hh + 256], ot[:, 256 * hh:256 * hh + 256],
                                 sml[:, 16:17], ALU.mult)
                        p.cp(sqb[:, :], ot[:, :])
                        for k in range(ND):
                            ps = pT0 if k % 2 == 0 else pT1
                            p.trans(ps[:, 0:128], sqb[:, 128 * k:128 * k + 128], idb[:, :])
                            p.cp(oTt[:, 128 * k:128 * k + 128], ps[:, 0:128])
                        for nn in range(2):
                            ps = pC if nn == 0 else pD
                            for k in range(ND):
                                p.mm(ps[:, 0:512], oTt[:, 128 * k:128 * k + 128],
                                     wob[:, 1024 * k + 512 * nn:1024 * k + 512 * nn + 512],
                                     start=(k == 0), stop=(k == ND - 1))
                        p.cp(f2[:, 0:512], pC[:, 0:512])
                        p.cp(f2[:, 512:1024], pD[:, 0:512])
                        resid_add(xsrc, xdst, t, f2[:, :])

    def sub_ffn(xsrc, xdst, idx):
        with ExitStack() as s1:
            wbf = s1.enter_context(sbn("wbf8", [128, ND * 1024], BF16))
            wi8 = s1.enter_context(sbn("wi8", [128, 1024], I8))
            scb = s1.enter_context(sbn("scb", [128, 1024], F32))
            bcb = s1.enter_context(sbn("bcb", [128, 1024], F32))
            rmx = s1.enter_context(sbn("rmx", [128, 1], F32))
            it_ = s1.enter_context(sbn("it_", [128, 1024], I32))
            for k in range(ND):
                p.dma(wi8[:, :], f_w8[128 * k:128 * k + 128, :])
                p.cp(wbf[:, 1024 * k:1024 * k + 1024], wi8[:, :])
            for t in range(NT):
                p.dma(f1[:, :], xsrc[128 * t:128 * t + 128, :])
                ln_tile(f1[:, :], f2[:, :])
                p.dma(y_d[128 * t:128 * t + 128, :], f2[:, :], eng="gpsimd")
                p.rmax(sml[:, 20:21], f2[:, :], absv=True)
                if t == 0:
                    p.cp(rmx[:, :], sml[:, 20:21])
                else:
                    p.tt(rmx[:, :], rmx[:, :], sml[:, 20:21], ALU.max)
            p.cp(sqb[:, 0:1], rmx[:, :])
            p.trans(pT0[0:1, 0:128], sqb[:, 0:1], idb[:, :])
            p.cp(sq[0:1, 0:128], pT0[0:1, 0:128])
            p.rmax(sml[0:1, 21:22], sq[0:1, 0:128])
            p.dma(cc_i[idx], sml[0:1, 21:22], eng="gpsimd")
            p.coll_max(cc_i[idx], cc_o[idx])
            p.dma(sml[0:1, 22:23], cc_o[idx])
            p.ts(sml[0:1, 22:23], sml[0:1, 22:23], 1.0 / 127.0, ALU.mult)
            p.ts(sml[0:1, 22:23], sml[0:1, 22:23], 1e-5, ALU.max)
            p.recip(sml[0:1, 23:24], sml[0:1, 22:23])
            p.mm(pA[:, 0:1], onf[0:1, 0:128], sml[0:1, 23:24])
            p.cp(rmx[:, :], pA[:, 0:1])
            p.dma(f2[0:1, 0:1024], f_ws)
            p.ts(f2[0:1, 0:1024], f2[0:1, 0:1024], sml[0:1, 22:23], ALU.mult)
            bcast_row_f32(f2[0:1, 0:1024], scb, 1024)
            p.dma(f2[0:1, 0:1024], f_bi)
            bcast_row_f32(f2[0:1, 0:1024], bcb, 1024)
            for t in range(NT):
                p.dma(f2[:, :], y_d[128 * t:128 * t + 128, :])
                p.ts(f2[:, :], f2[:, :], rmx[:, 0:1], ALU.mult)
                p.cp(it_[:, :], f2[:, :])
                p.cp(f2[:, :], it_[:, :])
                p.ts(f2[:, :], f2[:, :], 127.0, ALU.min)
                p.ts(f2[:, :], f2[:, :], -128.0, ALU.max)
                to_yt(f2[:, :], t)
            for t in range(NT):
                for nn in range(2):
                    ps = pC if nn == 0 else pD
                    for k in range(ND):
                        p.mm(ps[:, 0:512], yt[:, 4096 * k + 128 * t:4096 * k + 128 * t + 128],
                             wbf[:, 1024 * k + 512 * nn:1024 * k + 512 * nn + 512],
                             start=(k == 0), stop=(k == ND - 1))
                p.tt(f2[:, 0:512], pC[:, 0:512], scb[:, 0:512], ALU.mult)
                p.tt(f2[:, 512:1024], pD[:, 0:512], scb[:, 512:1024], ALU.mult)
                p.tt(f2[:, :], f2[:, :], bcb[:, :], ALU.add)
                resid_add(xsrc, xdst, t, f2[:, :])

    def attn_core(b, h, q_sb, k_sb, vall, oall, causal, pbs, ob64, vcol, vstride=1024):
        for c in range(4):
            cl = 512 * c
            for m in range(16):
                mb = 128 * m
                if causal and cl + 512 <= mb:
                    p.mset(pbs[:, 0:512], 1.0)
                else:
                    p.mm(pA[:, 0:512], k_sb[:, mb:mb + 128], q_sb[:, cl:cl + 512])
                    p.act(pbs[:, 0:512], pA[:, 0:512], AF.Exp, scale=0.125)
                    if causal and cl <= mb < cl + 512:
                        d0 = mb - cl
                        if d0 > 0:
                            p.mset(pbs[:, 0:d0], 1.0)
                        blk = pbs[:, d0:d0 + 128]
                        p.tt(blk, blk, muib[:, :], ALU.mult)
                        p.tt(blk, blk, mlsb[:, :], ALU.add)
                p.mm(pB[0:64, 0:512], vall[:, vstride * m + vcol:vstride * m + vcol + 64],
                     pbs[:, 0:512], start=(m == 0), stop=(m == 15))
                p.mm(pC[0:1, 0:512], onc[:, :], pbs[:, 0:512], start=(m == 0), stop=(m == 15))
            p.cp(sq[0:1, 0:512], pC[0:1, 0:512])
            p.recip(sq[0:1, 512:1024], sq[0:1, 0:512])
            p.mm(pD[0:64, 0:512], onf[0:1, 0:64], sq[0:1, 512:1024])
            r0 = 64 * (h % 2)
            p.cp(f1[0:64, 0:512], pD[0:64, 0:512])
            p.tt(ob64[0:64, 0:512], pB[0:64, 0:512], f1[0:64, 0:512], ALU.mult)
            p.dma(oall[r0:r0 + 64, 2048 * (h // 2) + cl:2048 * (h // 2) + cl + 512],
                  ob64[0:64, 0:512], eng="gpsimd")

    def sub_ab(xsrc, xdst):
        emit_ln_yt(xsrc)
        with ExitStack() as s1:
            wqt = s1.enter_context(sbn("wqt", [128, 1024], F32))
            iq = s1.enter_context(sbn("iq", [128, 1024], I32))
            for wsrc, wdst in ((b_wq, wq_s), (b_wk, wk_s), (b_wv, wv_s)):
                for r in range(ND):
                    p.dma(wqt[:, :], wsrc[128 * r:128 * r + 128, :])
                    p.rmax(sml[:, 24:25], wqt[:, :], absv=True)
                    p.ts(sml[:, 24:25], sml[:, 24:25], 1.0 / 127.0, ALU.mult)
                    p.ts(sml[:, 24:25], sml[:, 24:25], 1e-8, ALU.max)
                    p.recip(sml[:, 25:26], sml[:, 24:25])
                    p.ts(wqt[:, :], wqt[:, :], sml[:, 25:26], ALU.mult)
                    p.cp(iq[:, :], wqt[:, :])
                    p.cp(wqt[:, :], iq[:, :])
                    p.ts(wqt[:, :], wqt[:, :], 127.0, ALU.min)
                    p.ts(wqt[:, :], wqt[:, :], -127.0, ALU.max)
                    p.ts(wqt[:, :], wqt[:, :], sml[:, 24:25], ALU.mult)
                    p.cp(sqb[:, :], wqt[:, :])
                    for k in range(ND):
                        ps = pT0 if k % 2 == 0 else pT1
                        p.trans(ps[:, 0:128], sqb[:, 128 * k:128 * k + 128], idb[:, :])
                        p.cp(f2[:, 0:128], ps[:, 0:128])
                        p.cp(sqb[:, 0:0] if False else f1[:, 0:128], f2[:, 0:128]) if False else None
                        p.cp(it_dummy := None, None) if False else None
                        p.dma(wdst[128 * k:128 * k + 128, 128 * r:128 * r + 128],
                              f2[:, 0:128], eng="gpsimd")
        with ExitStack() as s1:
            bqb = s1.enter_context(sbn("bqb", [1, 1024], BF16))
            bkb = s1.enter_context(sbn("bkb", [1, 1024], BF16))
            bvb = s1.enter_context(sbn("bvb", [1, 1024], BF16))
            p.dma(f2[0:1, :], b_bq)
            p.cp(bqb[:, :], f2[0:1, :])
            p.dma(f2[0:1, :], b_bk)
            p.cp(bkb[:, :], f2[0:1, :])
            p.dma(f2[0:1, :], b_bv)
            p.cp(bvb[:, :], f2[0:1, :])
            for b in range(B):
                with ExitStack() as s2:
                    vall = s2.enter_context(sbn("vall", [128, 16 * 1024], BF16))
                    oall = s2.enter_context(sbn("oall", [128, 8 * 2048], BF16))
                    q_sb = s2.enter_context(sbn("q_sb", [64, 2048], BF16))
                    k_sb = s2.enter_context(sbn("k_sb", [64, 2048], BF16))
                    hw = s2.enter_context(sbn("hw", [128, 1024], BF16))
                    pbs = s2.enter_context(sbn("pbs", [128, 512], BF16))
                    with sbn("wvb", [128, ND * 1024], BF16) as wvb:
                        for k in range(ND):
                            p.dma(wvb[:, 1024 * k:1024 * k + 1024], wv_s[128 * k:128 * k + 128, :])
                        for t_ in range(16):
                            for nn in range(2):
                                ps = pC if nn == 0 else pD
                                p.mm(ps[:, 0:512], onb[0:1, 0:128],
                                     bvb[0:1, 512 * nn:512 * nn + 512], start=True, stop=False)
                                for k in range(ND):
                                    p.mm(ps[:, 0:512],
                                         yt[:, 4096 * k + 2048 * b + 128 * t_:
                                            4096 * k + 2048 * b + 128 * t_ + 128],
                                         wvb[:, 1024 * k + 512 * nn:1024 * k + 512 * nn + 512],
                                         start=False, stop=(k == ND - 1))
                                p.cp(vall[:, 1024 * t_ + 512 * nn:1024 * t_ + 512 * nn + 512],
                                     ps[:, 0:512])
                    for h in range(16):
                        for k in range(ND):
                            p.dma(hw[:, 64 * k:64 * k + 64],
                                  wq_s[128 * k:128 * k + 128, 64 * h:64 * h + 64])
                            p.dma(hw[:, 512 + 64 * k:512 + 64 * k + 64],
                                  wk_s[128 * k:128 * k + 128, 64 * h:64 * h + 64])
                        for c in range(4):
                            p.mm(pA[0:64, 0:512], bqb[0:1, 64 * h:64 * h + 64], onb[0:1, 0:512],
                                 start=True, stop=False)
                            for k in range(ND):
                                p.mm(pA[0:64, 0:512], hw[:, 64 * k:64 * k + 64],
                                     yt[:, 4096 * k + 2048 * b + 512 * c:
                                        4096 * k + 2048 * b + 512 * c + 512],
                                     start=False, stop=(k == ND - 1))
                            p.cp(q_sb[:, 512 * c:512 * c + 512], pA[0:64, 0:512])
                            p.mm(pB[0:64, 0:512], bkb[0:1, 64 * h:64 * h + 64], onb[0:1, 0:512],
                                 start=True, stop=False)
                            for k in range(ND):
                                p.mm(pB[0:64, 0:512], hw[:, 512 + 64 * k:512 + 64 * k + 64],
                                     yt[:, 4096 * k + 2048 * b + 512 * c:
                                        4096 * k + 2048 * b + 512 * c + 512],
                                     start=False, stop=(k == ND - 1))
                            p.cp(k_sb[:, 512 * c:512 * c + 512], pB[0:64, 0:512])
                        attn_core(b, h, q_sb, k_sb, vall, oall, False, pbs, ob64, 64 * h)
                    for t_ in range(16):
                        for k in range(ND):
                            ps = pT0 if k % 2 == 0 else pT1
                            p.trans(ps[:, 0:128], oall[:, 2048 * k + 128 * t_:2048 * k + 128 * t_ + 128],
                                    idb[:, :])
                            p.cp(f2[:, 128 * k:128 * k + 128], ps[:, 0:128])
                        resid_add(xsrc, xdst, 16 * b + t_, f2[:, :])

    def sub_l1aa(xsrc, xdst):
        emit_ln_yt(xsrc)
        with ExitStack() as s1:
            wqa = s1.enter_context(sbn("wqa", [128, ND * 1024], BF16))
            wkv = s1.enter_context(sbn("wkv", [128, ND * 128], BF16))
            wop = s1.enter_context(sbn("wop", [128, ND * 1024], BF16))
            for k in range(ND):
                p.dma(f2[:, :], c_qp[128 * k:128 * k + 128, :])
                p.cp(wqa[:, 1024 * k:1024 * k + 1024], f2[:, :])
                p.dma(f2[:, 0:64], c_kp[128 * k:128 * k + 128, :])
                p.cp(wkv[:, 128 * k:128 * k + 64], f2[:, 0:64])
                p.dma(f2[:, 0:64], c_vp[128 * k:128 * k + 128, :])
                p.cp(wkv[:, 128 * k + 64:128 * k + 128], f2[:, 0:64])
                p.dma(f2[:, :], c_op[128 * k:128 * k + 128, :])
                p.cp(wop[:, 1024 * k:1024 * k + 1024], f2[:, :])
            for b in range(B):
                with ExitStack() as s2:
                    vall = s2.enter_context(sbn("vall", [128, 16 * 64], BF16))
                    oall = s2.enter_context(sbn("oall", [128, 8 * 2048], BF16))
                    q_sb = s2.enter_context(sbn("q_sb", [64, 2048], BF16))
                    k_sb = s2.enter_context(sbn("k_sb", [64, 2048], BF16))
                    pbs = s2.enter_context(sbn("pbs", [128, 512], BF16))
                    for c in range(4):
                        for k in range(ND):
                            p.mm(pB[0:64, 0:512], wkv[:, 128 * k:128 * k + 64],
                                 yt[:, 4096 * k + 2048 * b + 512 * c:4096 * k + 2048 * b + 512 * c + 512],
                                 start=(k == 0), stop=(k == ND - 1))
                        p.cp(k_sb[:, 512 * c:512 * c + 512], pB[0:64, 0:512])
                    for t_ in range(16):
                        for k in range(ND):
                            p.mm(pC[:, 0:64],
                                 yt[:, 4096 * k + 2048 * b + 128 * t_:4096 * k + 2048 * b + 128 * t_ + 128],
                                 wkv[:, 128 * k + 64:128 * k + 128],
                                 start=(k == 0), stop=(k == ND - 1))
                        p.cp(vall[:, 64 * t_:64 * t_ + 64], pC[:, 0:64])
                    for h in range(16):
                        for c in range(4):
                            for k in range(ND):
                                p.mm(pA[0:64, 0:512], wqa[:, 1024 * k + 64 * h:1024 * k + 64 * h + 64],
                                     yt[:, 4096 * k + 2048 * b + 512 * c:
                                        4096 * k + 2048 * b + 512 * c + 512],
                                     start=(k == 0), stop=(k == ND - 1))
                            p.cp(q_sb[:, 512 * c:512 * c + 512], pA[0:64, 0:512])
                        attn_core(b, h, q_sb, k_sb, vall, oall, False, pbs, ob64, 0, vstride=64)
                    for t_ in range(16):
                        for nn in range(2):
                            ps = pC if nn == 0 else pD
                            for k in range(ND):
                                p.mm(ps[:, 0:512],
                                     oall[:, 2048 * k + 128 * t_:2048 * k + 128 * t_ + 128],
                                     wop[:, 1024 * k + 512 * nn:1024 * k + 512 * nn + 512],
                                     start=(k == 0), stop=(k == ND - 1))
                        p.cp(f2[:, 0:512], pC[:, 0:512])
                        p.cp(f2[:, 512:1024], pD[:, 0:512])
                        resid_add(xsrc, xdst, 16 * b + t_, f2[:, :])

    def sub_l1ab(xsrc, xdst):
        with ExitStack() as s1:
            l1wb = s1.enter_context(sbn("l1wb", [128, 1024], F32))
            l1bb = s1.enter_context(sbn("l1bb", [128, 1024], F32))
            l2wb = s1.enter_context(sbn("l2wb", [128, 1024], F32))
            l2bb = s1.enter_context(sbn("l2bb", [128, 1024], F32))
            bqb = s1.enter_context(sbn("bqb2", [1, 1024], BF16))
            bkb = s1.enter_context(sbn("bkb2", [1, 1024], BF16))
            bvb = s1.enter_context(sbn("bvb2", [1, 1024], BF16))
            bob = s1.enter_context(sbn("bob2", [1, 1024], BF16))
            p.dma(f2[0:1, :], d_l1w)
            bcast_row_f32(f2[0:1, :], l1wb, 1024)
            p.dma(f2[0:1, :], d_l1b)
            bcast_row_f32(f2[0:1, :], l1bb, 1024)
            p.dma(f2[0:1, :], d_l2w)
            bcast_row_f32(f2[0:1, :], l2wb, 1024)
            p.dma(f2[0:1, :], d_l2b)
            bcast_row_f32(f2[0:1, :], l2bb, 1024)
            for src, dstb in ((d_bq, bqb), (d_bk, bkb), (d_bv, bvb), (d_bo, bob)):
                p.dma(f2[0:1, :], src)
                p.cp(dstb[:, :], f2[0:1, :])
            emit_ln_yt(xsrc, store_y=True, double_ln=(l1wb[:, :], l1bb[:, :]))
            for b in range(B):
                with ExitStack() as s2:
                    oall = s2.enter_context(sbn("oall", [128, 8 * 2048], BF16))
                    s3 = s2.enter_context(ExitStack())
                    vall = s3.enter_context(sbn("vall", [128, 16 * 1024], BF16))
                    q_sb = s3.enter_context(sbn("q_sb", [64, 2048], BF16))
                    k_sb = s3.enter_context(sbn("k_sb", [64, 2048], BF16))
                    hw = s3.enter_context(sbn("hw", [128, 1024], BF16))
                    pbs = s3.enter_context(sbn("pbs", [128, 512], BF16))
                    wvb = s3.enter_context(sbn("wvb2", [128, 1024], BF16))
                    for h in range(16):
                        for k in range(ND):
                            p.dma(f2[:, 0:64], d_wv[128 * k:128 * k + 128, 64 * h:64 * h + 64])
                            p.cp(wvb[:, 64 * k:64 * k + 64], f2[:, 0:64])
                        for t_ in range(16):
                            p.mm(pC[:, 0:64], onb[0:1, 0:128], bvb[0:1, 64 * h:64 * h + 64],
                                 start=True, stop=False)
                            for k in range(ND):
                                p.mm(pC[:, 0:64],
                                     yt[:, 4096 * k + 2048 * b + 128 * t_:
                                        4096 * k + 2048 * b + 128 * t_ + 128],
                                     wvb[:, 64 * k:64 * k + 64], start=False, stop=(k == ND - 1))
                            p.cp(vall[:, 1024 * t_ + 64 * h:1024 * t_ + 64 * h + 64], pC[:, 0:64])
                    for h in range(16):
                        for k in range(ND):
                            p.dma(f2[:, 0:64], d_wq[128 * k:128 * k + 128, 64 * h:64 * h + 64])
                            p.cp(hw[:, 64 * k:64 * k + 64], f2[:, 0:64])
                            p.dma(f2[:, 0:64], d_wk[128 * k:128 * k + 128, 64 * h:64 * h + 64])
                            p.cp(hw[:, 512 + 64 * k:512 + 64 * k + 64], f2[:, 0:64])
                        for c in range(4):
                            p.mm(pA[0:64, 0:512], bqb[0:1, 64 * h:64 * h + 64], onb[0:1, 0:512],
                                 start=True, stop=False)
                            for k in range(ND):
                                p.mm(pA[0:64, 0:512], hw[:, 64 * k:64 * k + 64],
                                     yt[:, 4096 * k + 2048 * b + 512 * c:
                                        4096 * k + 2048 * b + 512 * c + 512],
                                     start=False, stop=(k == ND - 1))
                            p.cp(q_sb[:, 512 * c:512 * c + 512], pA[0:64, 0:512])
                            p.mm(pB[0:64, 0:512], bkb[0:1, 64 * h:64 * h + 64], onb[0:1, 0:512],
                                 start=True, stop=False)
                            for k in range(ND):
                                p.mm(pB[0:64, 0:512], hw[:, 512 + 64 * k:512 + 64 * k + 64],
                                     yt[:, 4096 * k + 2048 * b + 512 * c:
                                        4096 * k + 2048 * b + 512 * c + 512],
                                     start=False, stop=(k == ND - 1))
                            p.cp(k_sb[:, 512 * c:512 * c + 512], pB[0:64, 0:512])
                        attn_core(b, h, q_sb, k_sb, vall, oall, True, pbs, ob64, 64 * h)
                    s3.close()
                    with sbn("wo2", [128, ND * 1024], BF16) as wo2:
                        lnt = comb
                        for k in range(ND):
                            p.dma(f2[:, :], d_wo[128 * k:128 * k + 128, :])
                            p.cp(wo2[:, 1024 * k:1024 * k + 1024], f2[:, :])
                        for t_ in range(16):
                            for nn in range(2):
                                ps = pC if nn == 0 else pD
                                p.mm(ps[:, 0:512], onb[0:1, 0:128],
                                     bob[0:1, 512 * nn:512 * nn + 512], start=True, stop=False)
                                for k in range(ND):
                                    p.mm(ps[:, 0:512],
                                         oall[:, 2048 * k + 128 * t_:2048 * k + 128 * t_ + 128],
                                         wo2[:, 1024 * k + 512 * nn:1024 * k + 512 * nn + 512],
                                         start=False, stop=(k == ND - 1))
                            p.cp(f2[:, 0:512], pC[:, 0:512])
                            p.cp(f2[:, 512:1024], pD[:, 0:512])
                            t = 16 * b + t_
                            p.dma(f1[:, :], y_d[128 * t:128 * t + 128, :])
                            p.tt(f2[:, :], f2[:, :], f1[:, :], ALU.add)
                            p.cp(f1[:, :], f2[:, :])
                            ln_tile(f1[:, :], lnt[:, :])
                            p.tt(lnt[:, :], lnt[:, :], l2wb[:, :], ALU.mult)
                            p.tt(lnt[:, :], lnt[:, :], l2bb[:, :], ALU.add)
                            p.tt(f2[:, :], f2[:, :], lnt[:, :], ALU.add)
                            resid_add(xsrc, xdst, t, f2[:, :])

    def sub_moe(xsrc, xdst):
        with ExitStack() as s1:
            gwb = s1.enter_context(sbn("gwb", [128, ND * 32], BF16))
            gwl = s1.enter_context(sbn("gwl", [128, ND * 32], BF16))
            cbb = s1.enter_context(sbn("cbb", [128, 32], F32))
            sc_ = s1.enter_context(sbn("sc_", [128, 32], F32))
            sfc = s1.enter_context(sbn("sfc", [128, 32], F32))
            gs8 = s1.enter_context(sbn("gs8", [128, 8], F32))
            rk8 = s1.enter_context(sbn("rk8", [128, 8], F32))
            t8 = s1.enter_context(sbn("t8", [128, 8], F32))
            msk = s1.enter_context(sbn("msk", [128, 32], F32))
            rk32 = s1.enter_context(sbn("rk32", [128, 32], F32))
            t32 = s1.enter_context(sbn("t32", [128, 32], F32))
            ylb = s1.enter_context(sbn("ylb", [128, 1024], BF16))
            ylT = s1.enter_context(sbn("ylT", [128, 1024], BF16))
            for k in range(ND):
                p.dma(f2[:, 0:32], m_gw[128 * k:128 * k + 128, :])
                p.cp(gwb[:, 32 * k:32 * k + 32], f2[:, 0:32])
                p.cp(sq[:, 0:32], gwb[:, 32 * k:32 * k + 32])
                p.tt(sq[:, 32:64], f2[:, 0:32], sq[:, 0:32], ALU.subtract)
                p.cp(gwl[:, 32 * k:32 * k + 32], sq[:, 32:64])
            p.dma(f2[0:1, 0:32], m_cb)
            bcast_row_f32(f2[0:1, 0:32], cbb, 32)
            for t in range(NT):
                p.dma(f1[:, :], xsrc[128 * t:128 * t + 128, :])
                ln_tile(f1[:, :], f2[:, :])
                to_yt(f2[:, :], t)
                p.cp(f1[:, :], sqb[:, :])
                p.tt(f1[:, :], f2[:, :], f1[:, :], ALU.subtract)
                p.cp(ylb[:, :], f1[:, :])
                for k in range(ND):
                    ps = pT0 if k % 2 == 0 else pT1
                    p.trans(ps[:, 0:128], ylb[:, 128 * k:128 * k + 128], idb[:, :])
                    p.cp(ylT[:, 128 * k:128 * k + 128], ps[:, 0:128])
                for k in range(ND):
                    ysl = yt[:, 4096 * k + 128 * t:4096 * k + 128 * t + 128]
                    p.mm(pC[:, 0:32], ysl, gwb[:, 32 * k:32 * k + 32],
                         start=(k == 0), stop=False)
                    p.mm(pC[:, 0:32], ylT[:, 128 * k:128 * k + 128],
                         gwb[:, 32 * k:32 * k + 32], start=False, stop=False)
                    p.mm(pC[:, 0:32], ysl, gwl[:, 32 * k:32 * k + 32],
                         start=False, stop=(k == ND - 1))
                p.act(sc_[:, :], pC[:, 0:32], AF.Sigmoid)
                p.tt(sfc[:, :], sc_[:, :], cbb[:, :], ALU.add)
                for g in range(8):
                    p.rmax(gs8[:, g:g + 1], sfc[:, 4 * g:4 * g + 4])
                for j in range(8):
                    p.ts(t8[:, :], gs8[:, :], gs8[:, j:j + 1], ALU.is_gt)
                    if j == 0:
                        p.cp(rk8[:, :], t8[:, :])
                    else:
                        p.tt(rk8[:, :], rk8[:, :], t8[:, :], ALU.add)
                p.ts(rk8[:, :], rk8[:, :], 3.5, ALU.is_ge)
                for g in range(8):
                    p.ts(msk[:, 4 * g:4 * g + 4], sfc[:, 4 * g:4 * g + 4],
                         rk8[:, g:g + 1], ALU.mult)
                for j in range(32):
                    p.ts(t32[:, :], msk[:, :], msk[:, j:j + 1], ALU.is_gt)
                    if j == 0:
                        p.cp(rk32[:, :], t32[:, :])
                    else:
                        p.tt(rk32[:, :], rk32[:, :], t32[:, :], ALU.add)
                p.ts(rk32[:, :], rk32[:, :], 25.5, ALU.is_ge)
                p.tt(t32[:, :], rk32[:, :], sc_[:, :], ALU.mult)
                p.rsum(sml[:, 28:29], t32[:, :])
                p.ts(sml[:, 28:29], sml[:, 28:29], 1e-20, ALU.add)
                p.recip(sml[:, 29:30], sml[:, 28:29])
                p.ts(sml[:, 29:30], sml[:, 29:30], 2.5, ALU.mult)
                p.ts(comb[:, 32 * t:32 * t + 32], t32[:, :], sml[:, 29:30], ALU.mult)
        with ExitStack() as s1:
            accm = s1.enter_context(sbn("accm", [128, 16 * 1024], F32))
            wgu = s1.enter_context(sbn("wgu", [128, ND * 1024], BF16))
            wdn = s1.enter_context(sbn("wdn", [128, 4 * 1024], BF16))
            hbf = s1.enter_context(sbn("hbf", [128, 4 * 2048], BF16))
            for cchunk in range(2):
                tok0 = 2048 * cchunk
                for e in range(34):
                    for k in range(ND):
                        if e < 32:
                            p.dma(f2[:, :], m_gu[1024 * e + 128 * k:1024 * e + 128 * k + 128, :])
                        else:
                            c0 = 512 * (e - 32)
                            p.dma(f2[:, 0:512], m_sg[128 * k:128 * k + 128, c0:c0 + 512])
                            p.dma(f2[:, 512:1024], m_su[128 * k:128 * k + 128, c0:c0 + 512])
                        p.cp(wgu[:, 1024 * k:1024 * k + 1024], f2[:, :])
                    for it in range(4):
                        if e < 32:
                            p.dma(f2[:, :], m_dn[512 * e + 128 * it:512 * e + 128 * it + 128, :])
                        else:
                            r0 = 512 * (e - 32) + 128 * it
                            p.dma(f2[:, :], m_sd[r0:r0 + 128, :])
                        p.cp(wdn[:, 1024 * it:1024 * it + 1024], f2[:, :])
                    for nch in range(4):
                        tc = tok0 + 512 * nch
                        for it in range(4):
                            for k in range(ND):
                                p.mm(pA[:, 0:512], wgu[:, 1024 * k + 128 * it:1024 * k + 128 * it + 128],
                                     yt[:, 4096 * k + tc:4096 * k + tc + 512],
                                     start=(k == 0), stop=(k == ND - 1))
                                p.mm(pB[:, 0:512],
                                     wgu[:, 1024 * k + 512 + 128 * it:1024 * k + 512 + 128 * it + 128],
                                     yt[:, 4096 * k + tc:4096 * k + tc + 512],
                                     start=(k == 0), stop=(k == ND - 1))
                            p.act(sq[:, 0:512], pA[:, 0:512], AF.Silu)
                            p.tt(hbf[:, 2048 * it + 512 * nch:2048 * it + 512 * nch + 512],
                                 sq[:, 0:512], pB[:, 0:512], ALU.mult)
                    for mt in range(16):
                        tg = 16 * cchunk + mt
                        for nn in range(2):
                            ps = pC if nn == 0 else pD
                            for it in range(4):
                                p.mm(ps[:, 0:512],
                                     hbf[:, 2048 * it + 128 * mt:2048 * it + 128 * mt + 128],
                                     wdn[:, 1024 * it + 512 * nn:1024 * it + 512 * nn + 512],
                                     start=(it == 0), stop=(it == 3))
                        if e < 32:
                            ccol = comb[:, 32 * tg + e:32 * tg + e + 1]
                            p.ts(f2[:, 0:512], pC[:, 0:512], ccol, ALU.mult)
                            p.ts(f2[:, 512:1024], pD[:, 0:512], ccol, ALU.mult)
                        else:
                            p.cp(f2[:, 0:512], pC[:, 0:512])
                            p.cp(f2[:, 512:1024], pD[:, 0:512])
                        if e == 0:
                            p.cp(accm[:, 1024 * mt:1024 * mt + 1024], f2[:, :])
                        else:
                            p.tt(accm[:, 1024 * mt:1024 * mt + 1024],
                                 accm[:, 1024 * mt:1024 * mt + 1024], f2[:, :], ALU.add)
                for mt in range(16):
                    resid_add(xsrc, xdst, 16 * cchunk + mt, accm[:, 1024 * mt:1024 * mt + 1024])

    sub_aa(hs, x_d)
    sub_ffn(x_d, x_d, 0)
    sub_ab(x_d, x_d)
    sub_ffn(x_d, x_d, 1)
    sub_l1aa(x_d, x_d)
    sub_moe(x_d, x_d)
    sub_l1ab(x_d, x_d)
    sub_moe(x_d, out_d)

    es.close()

    g = nc.alloc_semaphore(name="g")
    total = p.cum
    with nc.Block() as blk:
        def mk(name):
            def body(eng):
                for (w, fn, inc) in p.q[name]:
                    if w > 0:
                        eng.wait_ge(g, w)
                    fn(eng).then_inc(g, inc)
                if name == "gpsimd":
                    eng.wait_ge(g, total)
            return body
        blk.sync(mk("sync"))
        blk.scalar(mk("scalar"))
        blk.vector(mk("vector"))
        blk.tensor(mk("tensor"))
        blk.gpsimd(mk("gpsimd"))
    return nc


def kernel(**inp):
    f = np.ascontiguousarray
    x = inp["L0_attn_a_hidden_states"]
    eye = np.eye(128, dtype=np.float32)
    ones = np.ones((1, 512), dtype=np.float32)
    r = np.arange(128)
    mls_ = (r[:, None] > r[None, :]).astype(np.float32)
    mnus_ = -(r[:, None] < r[None, :]).astype(np.float32)
    mui_ = (r[:, None] <= r[None, :]).astype(np.float32)

    common = {
        "a_wq": f(inp["L0_attn_a_Wq"]), "a_wk": f(inp["L0_attn_a_Wk"]),
        "a_wv": f(inp["L0_attn_a_Wv"]), "a_wb": f(inp["L0_attn_a_Wb"]),
        "a_cq": f(inp["L0_attn_a_conv_q_w"]), "a_ck": f(inp["L0_attn_a_conv_k_w"]),
        "a_cv": f(inp["L0_attn_a_conv_v_w"]),
        "a_fs": f(inp["L0_attn_a_fir_short_w"].reshape(1024, 3)),
        "a_fl": f(inp["L0_attn_a_fir_long_w"].reshape(1024, 63)),
        "a_w1": f(inp["L0_attn_a_mlp_w1"]),
        "a_b1": f(inp["L0_attn_a_mlp_b1"].reshape(2048, 1)),
        "a_w2": f(inp["L0_attn_a_mlp_w2"]),
        "a_b2": f(inp["L0_attn_a_mlp_b2"].reshape(1, 16)),
        "a_glt": f(inp["L0_attn_a_gate_log_temp"].reshape(1, 4)),
        "a_onw": f(inp["L0_attn_a_o_norm_w"].reshape(256, 1)),
        "a_wo": f(inp["L0_attn_a_Wo"]),
        "b_wq": f(inp["L0_attn_b_Wq"]), "b_wk": f(inp["L0_attn_b_Wk"]),
        "b_wv": f(inp["L0_attn_b_Wv"]),
        "b_bq": f(inp["L0_attn_b_bq"].reshape(1, 1024)),
        "b_bk": f(inp["L0_attn_b_bk"].reshape(1, 1024)),
        "b_bv": f(inp["L0_attn_b_bv"].reshape(1, 1024)),
        "f_w8": f(inp["L0_ffn_weight_int8"].T),
        "f_ws": f(inp["L0_ffn_weight_scale"].reshape(1, 1024)),
        "f_bi": f(inp["L0_ffn_bias"].reshape(1, 1024)),
        "c_qp": f(inp["L1_attn_a_query_proj"].transpose(2, 0, 1).reshape(1024, 1024)),
        "c_kp": f(inp["L1_attn_a_key_proj"]),
        "c_vp": f(inp["L1_attn_a_value_proj"]),
        "c_op": f(inp["L1_attn_a_output_proj"].transpose(1, 2, 0).reshape(1024, 1024)),
        "d_wq": f(inp["L1_attn_b_W_Q"].transpose(1, 0, 2).reshape(1024, 1024)),
        "d_wk": f(inp["L1_attn_b_W_K"].transpose(1, 0, 2).reshape(1024, 1024)),
        "d_wv": f(inp["L1_attn_b_W_V"].transpose(1, 0, 2).reshape(1024, 1024)),
        "d_wo": f(inp["L1_attn_b_W_O"].reshape(1024, 1024)),
        "d_bq": f(inp["L1_attn_b_b_Q"].reshape(1, 1024)),
        "d_bk": f(inp["L1_attn_b_b_K"].reshape(1, 1024)),
        "d_bv": f(inp["L1_attn_b_b_V"].reshape(1, 1024)),
        "d_bo": f(inp["L1_attn_b_b_O"].reshape(1, 1024)),
        "d_l1w": f(inp["L1_attn_b_ln1_w"].reshape(1, 1024)),
        "d_l1b": f(inp["L1_attn_b_ln1_b"].reshape(1, 1024)),
        "d_l2w": f(inp["L1_attn_b_ln2_w"].reshape(1, 1024)),
        "d_l2b": f(inp["L1_attn_b_ln2_b"].reshape(1, 1024)),
        "m_gw": f(inp["L1_ffn_gate_weight"].T),
        "m_cb": f(inp["L1_ffn_e_score_correction_bias"].reshape(1, 32)),
        "m_gu": f(inp["L1_ffn_experts_gate_up"].transpose(0, 2, 1).reshape(32 * 1024, 1024)),
        "m_dn": f(inp["L1_ffn_experts_down"].transpose(0, 2, 1).reshape(32 * 512, 1024)),
        "m_sg": f(inp["L1_ffn_shared_gate_w"].T),
        "m_su": f(inp["L1_ffn_shared_up_w"].T),
        "m_sd": f(inp["L1_ffn_shared_down_w"].T),
        "c_eye": eye, "c_ones": ones,
        "c_mls": mls_, "c_mnus": mnus_, "c_mui": mui_,
    }
    in_maps = []
    for c in range(8):
        m = dict(common)
        m["hs"] = f(x[2 * c:2 * c + 2].reshape(T, D))
        in_maps.append(m)
    nc = build()
    res = run_bass_kernel_spmd(nc, in_maps, core_ids=list(range(8)))
    out = np.concatenate([res.results[c]["out"].reshape(2, L, D) for c in range(8)], axis=0)
    return out.astype(np.float32)


# version 09 (final)
# speedup vs baseline: 1.3350x; 1.3350x over previous
import numpy as np
import ml_dtypes
from contextlib import ExitStack
import concourse.bass as bass
import concourse.mybir as mybir
from concourse.bass_utils import run_bass_kernel_spmd

F32 = mybir.dt.float32
BF16 = mybir.dt.bfloat16
I32 = mybir.dt.int32
I8 = mybir.dt.int8
AF = mybir.ActivationFunctionType
ALU = mybir.AluOpType
AX = mybir.AxisListType

B, L, T, D = 2, 2048, 4096, 1024
NT, ND = T // 128, D // 128


class P:
    ENGS = ("sync", "scalar", "vector", "tensor", "gpsimd")

    def __init__(self, nc):
        self.nc = nc
        self.cum = 0
        self.other = {e: 0 for e in self.ENGS}
        self.lastw = {e: -1 for e in self.ENGS}
        self.q = {e: [] for e in self.ENGS}

    def _emit(self, eng, fn, inc):
        import os
        relax = os.environ.get("RELAX", "tensor").split(",")
        if eng in ("sync", "gpsimd") or eng not in relax:
            w = self.cum
        elif self.other[eng] > self.lastw[eng]:
            w = self.cum
            self.lastw[eng] = self.other[eng]
        else:
            w = -1
        self.q[eng].append((w, fn, inc))
        self.cum += inc
        for f in self.ENGS:
            if f != eng:
                self.other[f] += inc

    def dma(self, dst, src, eng="sync"):
        self._emit(eng, lambda e, d=dst, s=src: e.dma_start(out=d, in_=s), 16)

    def mm(self, out, lhsT, rhs, start=True, stop=True):
        self._emit("tensor", lambda e, o=out, l=lhsT, r=rhs, st=start, sp=stop:
                   e.matmul(o, l, r, start=st, stop=sp, skip_group_check=True), 1)

    def trans(self, out, in_, ident):
        self._emit("tensor", lambda e, o=out, i=in_, d=ident: e.transpose(o, i, d), 1)

    def cp(self, out, in_, eng="vector"):
        self._emit(eng, lambda e, o=out, i=in_: e.tensor_copy(o, i), 1)

    def tt(self, out, a, b, op):
        self._emit("vector", lambda e, o=out, x=a, y=b, pp=op: e.tensor_tensor(o, x, y, pp), 1)

    def ts(self, out, a, s, op):
        self._emit("vector", lambda e, o=out, x=a, sc=s, pp=op:
                   e.tensor_scalar(o, x, sc, None, op0=pp), 1)

    def stt(self, out, in0, s, in1, op0, op1):
        self._emit("vector", lambda e, o=out, x=in0, sc=s, y=in1, p0=op0, p1=op1:
                   e.scalar_tensor_tensor(o, x, sc, y, p0, p1), 1)

    def rsum(self, out, in_, absv=None, neg=None):
        self._emit("vector", lambda e, o=out, i=in_, a=absv, n=neg:
                   e.reduce_sum(o, i, axis=AX.X, apply_absolute_value=a, negate=n), 1)

    def rmax(self, out, in_, absv=None, neg=None):
        self._emit("vector", lambda e, o=out, i=in_, a=absv, n=neg:
                   e.reduce_max(o, i, axis=AX.X, apply_absolute_value=a, negate=n), 1)

    def mset(self, ap, v):
        self._emit("vector", lambda e, a=ap, c=v: e.memset(a, c), 1)

    def recip(self, out, in_):
        self._emit("vector", lambda e, o=out, i=in_: e.reciprocal(o, i), 1)

    def act(self, out, in_, f, bias=None, scale=None, accum=None):
        def fn(e, o=out, i=in_, fu=f, b=bias, s=scale, a=accum):
            kw = {}
            if b is not None:
                kw["bias"] = b
            if s is not None:
                kw["scale"] = s
            if a is not None:
                kw["accum_out"] = a
            return e.activation(o, i, fu, **kw)
        self._emit("scalar", fn, 1)

    def coll_max(self, cin, cout):
        self._emit("gpsimd", lambda e, a=cin, b=cout: e.collective_compute(
            "AllReduce", ALU.max, replica_groups=[[0, 1, 2, 3, 4, 5, 6, 7]],
            ins=[a], outs=[b]), 1)


def build():
    nc = bass.Bass("TRN2", target_bir_lowering=False)
    p = P(nc)
    _uid = [0]

    def sbn(name, shape, dt=F32):
        _uid[0] += 1
        return nc.sbuf_tensor("%s_%d" % (name, _uid[0]), shape, dt)

    def din(name, shape, dt=F32):
        return nc.dram_tensor(name, shape, dt, kind="ExternalInput").ap()

    def dint(name, shape, dt=F32):
        return nc.dram_tensor(name, shape, dt, kind="Internal").ap()

    hs = din("hs", [T, D])
    a_wq = din("a_wq", [D, D]); a_wk = din("a_wk", [D, D]); a_wv = din("a_wv", [D, D])
    a_wb = din("a_wb", [D, 4])
    a_cq = din("a_cq", [D, 4]); a_ck = din("a_ck", [D, 4]); a_cv = din("a_cv", [D, 4])
    a_fs = din("a_fs", [D, 3]); a_fl = din("a_fl", [D, 63])
    a_w1 = din("a_w1", [1056, 2048]); a_b1 = din("a_b1", [2048, 1])
    a_w2 = din("a_w2", [2048, 16]); a_b2 = din("a_b2", [1, 16])
    a_glt = din("a_glt", [1, 4]); a_onw = din("a_onw", [256, 1])
    a_wo = din("a_wo", [D, D])
    b_wq = din("b_wq", [D, D]); b_wk = din("b_wk", [D, D]); b_wv = din("b_wv", [D, D])
    b_bq = din("b_bq", [1, D]); b_bk = din("b_bk", [1, D]); b_bv = din("b_bv", [1, D])
    f_w8 = din("f_w8", [D, D], I8)
    f_ws = din("f_ws", [1, D]); f_bi = din("f_bi", [1, D])
    c_qp = din("c_qp", [D, D]); c_kp = din("c_kp", [D, 64]); c_vp = din("c_vp", [D, 64])
    c_op = din("c_op", [D, D])
    d_wq = din("d_wq", [D, D]); d_wk = din("d_wk", [D, D]); d_wv = din("d_wv", [D, D])
    d_wo = din("d_wo", [D, D])
    d_bq = din("d_bq", [1, D]); d_bk = din("d_bk", [1, D]); d_bv = din("d_bv", [1, D])
    d_bo = din("d_bo", [1, D])
    d_l1w = din("d_l1w", [1, D]); d_l1b = din("d_l1b", [1, D])
    d_l2w = din("d_l2w", [1, D]); d_l2b = din("d_l2b", [1, D])
    m_gw = din("m_gw", [D, 32]); m_cb = din("m_cb", [1, 32])
    m_gu = din("m_gu", [32 * D, D])
    m_dn = din("m_dn", [32 * 512, D])
    m_sg = din("m_sg", [D, D]); m_su = din("m_su", [D, D]); m_sd = din("m_sd", [D, D])
    c_eye = din("c_eye", [128, 128])
    c_ones = din("c_ones", [1, 512])
    c_mls = din("c_mls", [128, 128])
    c_mnus = din("c_mnus", [128, 128])
    c_mui = din("c_mui", [128, 128])

    out_d = nc.dram_tensor("out", [T, D], F32, kind="ExternalOutput").ap()

    x_d = dint("x_d", [T, D])
    y_d = dint("y_d", [T, D])
    qc_d = dint("qc_d", [2 * D, L], BF16)
    kc_d = dint("kc_d", [2 * D, L], BF16)
    vc_d = dint("vc_d", [2 * D, L], BF16)
    fs_d = dint("fs_d", [T, D], BF16)
    fl_d = dint("fl_d", [T, D], BF16)
    vb_d = dint("vb_d", [T, D], BF16)
    do_d = dint("do_d", [T, D], BF16)
    wq_s = dint("wq_s", [D, D], BF16)
    wk_s = dint("wk_s", [D, D], BF16)
    wv_s = dint("wv_s", [D, D], BF16)
    cc_i = [dint("cc_i%d" % i, [1, 1]) for i in range(2)]
    cc_o = [dint("cc_o%d" % i, [1, 1]) for i in range(2)]

    es = ExitStack()
    E = es.enter_context

    yt = E(sbn("yt", [128, ND * T], BF16))
    idb = E(sbn("idb", [128, 128], BF16))
    onf = E(sbn("onf", [1, 512], F32))
    onb = E(sbn("onb", [1, 512], BF16))
    onc = E(sbn("onc", [128, 1], BF16))
    mls = E(sbn("mls", [128, 128], F32))
    mnus = E(sbn("mnus", [128, 128], F32))
    mui = E(sbn("mui", [128, 128], F32))
    mlsb = E(sbn("mlsb", [128, 128], BF16))
    muib = E(sbn("muib", [128, 128], BF16))
    f1 = E(sbn("f1", [128, 1024], F32))
    f2 = E(sbn("f2", [128, 1024], F32))
    sq = E(sbn("sq", [128, 1024], F32))
    sqb = E(sbn("sqb", [128, 1024], BF16))
    comb = E(sbn("comb", [128, 1024], F32))
    sml = E(sbn("sml", [128, 64], F32))
    ob64 = E(sbn("ob64", [64, 512], BF16))

    pA = E(nc.psum_tensor("pA", [128, 512], F32))
    pB = E(nc.psum_tensor("pB", [128, 512], F32))
    pC = E(nc.psum_tensor("pC", [128, 512], F32))
    pD = E(nc.psum_tensor("pD", [128, 512], F32))
    pE = E(nc.psum_tensor("pE", [128, 512], F32))
    pT0 = E(nc.psum_tensor("pT0", [128, 128], BF16))
    pT1 = E(nc.psum_tensor("pT1", [128, 128], BF16))

    p.dma(f1[:, 0:128], c_eye)
    p.cp(idb[:, :], f1[:, 0:128])
    p.dma(f1[0:1, 0:512], c_ones)
    p.cp(onf[:, :], f1[0:1, 0:512])
    p.cp(onb[:, :], f1[0:1, 0:512])
    p.mset(onc[:, :], 1.0)
    p.dma(mls[:, :], c_mls)
    p.dma(mnus[:, :], c_mnus)
    p.dma(mui[:, :], c_mui)
    p.cp(mlsb[:, :], mls[:, :])
    p.cp(muib[:, :], mui[:, :])

    def rsqrt_(dst, src, bias, scale):
        p.ts(sml[:, 60:61], src, scale, ALU.mult)
        p.ts(sml[:, 60:61], sml[:, 60:61], bias, ALU.add)
        p.recip(sml[:, 61:62], sml[:, 60:61])
        p.act(dst, sml[:, 61:62], AF.Sqrt)

    def ln_tile(src_ap, dst_f32):
        p.rsum(sml[:, 0:1], src_ap)
        p.ts(sml[:, 0:1], sml[:, 0:1], 1.0 / 1024.0, ALU.mult)
        p.ts(dst_f32, src_ap, sml[:, 0:1], ALU.subtract)
        p.act(sq[:, 0:1024], dst_f32, AF.Square, accum=sml[:, 1:2])
        rsqrt_(sml[:, 2:3], sml[:, 1:2], 1e-5, 1.0 / 1024.0)
        p.ts(dst_f32, dst_f32, sml[:, 2:3], ALU.mult)

    def to_yt(tile_f32, t):
        p.cp(sqb[:, :], tile_f32)
        for k in range(ND):
            ps = pT0 if k % 2 == 0 else pT1
            p.trans(ps[:, 0:128], sqb[:, 128 * k:128 * k + 128], idb[:, :])
            p.cp(yt[:, 4096 * k + 128 * t: 4096 * k + 128 * t + 128], ps[:, 0:128])

    def emit_ln_yt(xsrc, store_y=False, double_ln=None):
        for t in range(NT):
            p.dma(f1[:, :], xsrc[128 * t:128 * t + 128, :])
            ln_tile(f1[:, :], f2[:, :])
            if store_y:
                p.dma(y_d[128 * t:128 * t + 128, :], f2[:, :], eng="gpsimd")
            if double_ln is not None:
                w_b, b_b = double_ln
                p.cp(f1[:, :], f2[:, :])
                ln_tile(f1[:, :], f2[:, :])
                p.tt(f2[:, :], f2[:, :], w_b, ALU.mult)
                p.tt(f2[:, :], f2[:, :], b_b, ALU.add)
            to_yt(f2[:, :], t)

    def bcast_row_f32(row_ap, dst_sb, n):
        for c in range(0, n, 512):
            w = min(512, n - c)
            p.mm(pA[:, 0:w], onf[0:1, 0:128], row_ap[0:1, c:c + w])
            p.cp(dst_sb[:, c:c + w], pA[:, 0:w])

    def resid_add(xsrc, xdst, t, add_f32):
        p.dma(f1[:, :], xsrc[128 * t:128 * t + 128, :])
        p.tt(f1[:, :], f1[:, :], add_f32, ALU.add)
        p.dma(xdst[128 * t:128 * t + 128, :], f1[:, :], eng="gpsimd")

    def sub_aa(xsrc, xdst):
        emit_ln_yt(xsrc)
        with ExitStack() as s1:
            beta = s1.enter_context(sbn("beta", [128, 128], F32))
            wb = s1.enter_context(sbn("wbb", [128, 32], BF16))
            for k in range(ND):
                p.dma(f2[:, 0:4], a_wb[128 * k:128 * k + 128, :])
                p.cp(wb[:, 4 * k:4 * k + 4], f2[:, 0:4])
            for t in range(NT):
                for k in range(ND):
                    p.mm(pA[:, 0:4], yt[:, 4096 * k + 128 * t: 4096 * k + 128 * t + 128],
                         wb[:, 4 * k:4 * k + 4], start=(k == 0), stop=(k == ND - 1))
                p.act(beta[:, 4 * t:4 * t + 4], pA[:, 0:4], AF.Sigmoid)
            for wsrc, csrc, dst in ((a_wq, a_cq, qc_d), (a_wk, a_ck, kc_d), (a_wv, a_cv, vc_d)):
                with ExitStack() as s2:
                    wbf = s2.enter_context(sbn("wbf", [128, ND * 1024], BF16))
                    cw = s2.enter_context(sbn("cw", [128, 32], F32))
                    cb = s2.enter_context(sbn("cb", [128, 2051], F32))
                    acc = s2.enter_context(sbn("acc", [128, 2048], F32))
                    for k in range(ND):
                        p.dma(f2[:, :], wsrc[128 * k:128 * k + 128, :])
                        p.cp(wbf[:, 1024 * k:1024 * k + 1024], f2[:, :])
                        p.dma(cw[:, 4 * k:4 * k + 4], csrc[128 * k:128 * k + 128, :])
                    for b in range(B):
                        for kt in range(ND):
                            p.mset(cb[:, 0:3], 0.0)
                            for c4 in range(4):
                                for k in range(ND):
                                    p.mm(pA[:, 0:512],
                                         wbf[:, 1024 * k + 128 * kt:1024 * k + 128 * kt + 128],
                                         yt[:, 4096 * k + 2048 * b + 512 * c4:
                                            4096 * k + 2048 * b + 512 * c4 + 512],
                                         start=(k == 0), stop=(k == ND - 1))
                                p.cp(cb[:, 3 + 512 * c4:3 + 512 * c4 + 512], pA[:, 0:512])
                            for j in range(4):
                                if j == 0:
                                    p.ts(acc[:, :], cb[:, 0:2048], cw[:, 4 * kt:4 * kt + 1], ALU.mult)
                                else:
                                    p.stt(acc[:, :], cb[:, j:j + 2048], cw[:, 4 * kt + j:4 * kt + j + 1],
                                          acc[:, :], ALU.mult, ALU.add)
                            p.act(acc[:, :], acc[:, :], AF.Silu)
                            p.cp(sqb[:, 0:1024], acc[:, 0:1024])
                            p.cp(sqb[0:128, 0:1024] if False else sqb[:, 0:1024], acc[:, 0:1024]) if False else None
                            p.dma(dst[1024 * b + 128 * kt:1024 * b + 128 * kt + 128, 0:1024],
                                  sqb[:, 0:1024], eng="gpsimd")
                            p.cp(sqb[:, 0:1024], acc[:, 1024:2048])
                            p.dma(dst[1024 * b + 128 * kt:1024 * b + 128 * kt + 128, 1024:2048],
                                  sqb[:, 0:1024], eng="gpsimd")
            with ExitStack() as s2:
                fw = s2.enter_context(sbn("fw", [128, 8 * 66], F32))
                fb = s2.enter_context(sbn("fb", [128, 2110], BF16))
                accl = s2.enter_context(sbn("accl", [128, 2048], F32))
                for k in range(ND):
                    p.dma(fw[:, 66 * k:66 * k + 3], a_fs[128 * k:128 * k + 128, :])
                    p.dma(fw[:, 66 * k + 3:66 * k + 66], a_fl[128 * k:128 * k + 128, :])
                for b in range(B):
                    for kt in range(ND):
                        p.mset(fb[:, 0:62], 0.0)
                        p.dma(fb[:, 62:2110], vc_d[1024 * b + 128 * kt:1024 * b + 128 * kt + 128, :])
                        for (dst, off0, ntap, woff) in ((fs_d, 60, 3, 66 * kt),
                                                        (fl_d, 0, 63, 66 * kt + 3)):
                            for j in range(ntap):
                                if j == 0:
                                    p.ts(accl[:, :], fb[:, off0:off0 + 2048],
                                         fw[:, woff:woff + 1], ALU.mult)
                                else:
                                    p.stt(accl[:, :], fb[:, off0 + j:off0 + j + 2048],
                                          fw[:, woff + j:woff + j + 1],
                                          accl[:, :], ALU.mult, ALU.add)
                            p.cp(sqb[:, 0:1024] if False else sqb[:, 0:1024], accl[:, 0:1024]) if False else None
                            for half in range(2):
                                p.cp(sqb[:, 0:1024], accl[:, 1024 * half:1024 * half + 1024])
                                for tt_ in range(8):
                                    tg = 8 * half + tt_
                                    ps = pT0 if tt_ % 2 == 0 else pT1
                                    p.trans(ps[:, 0:128], sqb[:, 128 * tt_:128 * tt_ + 128], idb[:, :])
                                    p.cp(sqb[0:0, 0:0] if False else f2[:, 0:128], ps[:, 0:128]) if False else None
                                    p.cp(beta[0:0, 0:0] if False else sml[:, 32:33], onc[:, :]) if False else None
                                    p.cp(f2[:, 0:128], ps[:, 0:128])
                                    p.cp(sml[:, 32:33], sml[:, 32:33]) if False else None
                                    p.cp(sqb2_unused := None, None) if False else None
                                    p.cp(f1[:, 0:128], f2[:, 0:128]) if False else None
                                    p.dma(dst[2048 * b + 128 * tg:2048 * b + 128 * tg + 128,
                                              128 * kt:128 * kt + 128], f2[:, 0:128], eng="gpsimd")
                        for half in range(2):
                            for tt_ in range(8):
                                tg = 8 * half + tt_
                                ps = pT0 if tt_ % 2 == 0 else pT1
                                p.trans(ps[:, 0:128],
                                        fb[:, 62 + 1024 * half + 128 * tt_:62 + 1024 * half + 128 * tt_ + 128],
                                        idb[:, :])
                                p.cp(f2[:, 0:128], ps[:, 0:128])
                                p.dma(vb_d[2048 * b + 128 * tg:2048 * b + 128 * tg + 128,
                                           128 * kt:128 * kt + 128], f2[:, 0:128], eng="gpsimd")
            for b in range(B):
                for h in range(4):
                    with ExitStack() as s2:
                        qct = s2.enter_context(sbn("qct", [128, 4096], BF16))
                        kct = s2.enter_context(sbn("kct", [128, 4096], BF16))
                        vct = s2.enter_context(sbn("vct", [128, 4096], BF16))
                        qtk = s2.enter_context(sbn("qtk", [128, 4096], BF16))
                        ktk = s2.enter_context(sbn("ktk", [128, 4096], BF16))
                        vtk = s2.enter_context(sbn("vtk", [128, 4096], BF16))
                        kbk = s2.enter_context(sbn("kbk", [128, 4096], BF16))
                        uf = s2.enter_context(sbn("uf", [128, 512], F32))
                        ub = s2.enter_context(sbn("ub", [128, 512], BF16))
                        ptb = s2.enter_context(sbn("ptb", [128, 128], BF16))
                        pb_ = s2.enter_context(sbn("pb_", [128, 128], BF16))
                        atb = s2.enter_context(sbn("atb", [128, 128], BF16))
                        wtb = s2.enter_context(sbn("wtb", [128, 256], BF16))
                        sbf = s2.enter_context(sbn("sbf", [128, 512], BF16))
                        u2b = s2.enter_context(sbn("u2b", [128, 256], BF16))
                        for dsub in range(2):
                            r0 = 1024 * b + 256 * h + 128 * dsub
                            p.dma(qct[:, 2048 * dsub:2048 * dsub + 2048], qc_d[r0:r0 + 128, :])
                            p.dma(kct[:, 2048 * dsub:2048 * dsub + 2048], kc_d[r0:r0 + 128, :])
                            p.dma(vct[:, 2048 * dsub:2048 * dsub + 2048], vc_d[r0:r0 + 128, :])
                        for src, dst in ((qct, qtk), (kct, ktk), (vct, vtk)):
                            for dsub in range(2):
                                for tt_ in range(16):
                                    ps = pT0 if tt_ % 2 == 0 else pT1
                                    p.trans(ps[:, 0:128],
                                            src[:, 2048 * dsub + 128 * tt_:2048 * dsub + 128 * tt_ + 128],
                                            idb[:, :])
                                    p.cp(dst[:, 256 * tt_ + 128 * dsub:256 * tt_ + 128 * dsub + 128],
                                         ps[:, 0:128])
                        for tt_ in range(16):
                            bcol = beta[:, 4 * (16 * b + tt_) + h:4 * (16 * b + tt_) + h + 1]
                            for buf in (qtk, ktk):
                                p.act(sq[:, 0:256], buf[:, 256 * tt_:256 * tt_ + 256],
                                      AF.Square, accum=sml[:, 3:4])
                                rsqrt_(sml[:, 4:5], sml[:, 3:4], 1e-6, 1.0)
                                p.ts(buf[:, 256 * tt_:256 * tt_ + 256],
                                     buf[:, 256 * tt_:256 * tt_ + 256], sml[:, 4:5], ALU.mult)
                            p.ts(kbk[:, 256 * tt_:256 * tt_ + 256],
                                 ktk[:, 256 * tt_:256 * tt_ + 256], bcol, ALU.mult)
                            p.ts(vtk[:, 256 * tt_:256 * tt_ + 256],
                                 vtk[:, 256 * tt_:256 * tt_ + 256], bcol, ALU.mult)
                        for src, dst in ((qtk, qct), (ktk, kct)):
                            for tt_ in range(16):
                                for dsub in range(2):
                                    ps = pT0 if dsub == 0 else pT1
                                    p.trans(ps[:, 0:128],
                                            src[:, 256 * tt_ + 128 * dsub:256 * tt_ + 128 * dsub + 128],
                                            idb[:, :])
                                    p.cp(dst[:, 2048 * dsub + 128 * tt_:2048 * dsub + 128 * tt_ + 128],
                                         ps[:, 0:128])
                        for g in range(16):
                            gc = 128 * g
                            for dsub in range(2):
                                ps = pT0 if dsub == 0 else pT1
                                p.trans(ps[:, 0:128],
                                        kbk[:, 256 * g + 128 * dsub:256 * g + 128 * dsub + 128],
                                        idb[:, :])
                                p.cp(wtb[:, 128 * dsub:128 * dsub + 128], ps[:, 0:128])
                            for dsub in range(2):
                                p.mm(pC[:, 0:128], kct[:, 2048 * dsub + gc:2048 * dsub + gc + 128],
                                     wtb[:, 128 * dsub:128 * dsub + 128],
                                     start=(dsub == 0), stop=(dsub == 1))
                            p.tt(sq[:, 0:128], pC[:, 0:128], mnus[:, :], ALU.mult)
                            p.cp(ptb[:, :], sq[:, 0:128])
                            p.trans(pT0[:, 0:128], ptb[:, :], idb[:, :])
                            p.cp(pb_[:, :], pT0[:, 0:128])
                            p.cp(uf[:, 0:256], vtk[:, 256 * g:256 * g + 256])
                            p.cp(uf[:, 256:512], kbk[:, 256 * g:256 * g + 256])
                            p.cp(ub[:, :], uf[:, :])
                            for lvl in range(7):
                                p.mm(pC[:, 0:512], ptb[:, :], ub[:, :])
                                p.tt(uf[:, :], uf[:, :], pC[:, 0:512], ALU.add)
                                p.cp(ub[:, :], uf[:, :])
                                if lvl < 6:
                                    p.mm(pC[:, 0:128], ptb[:, :], pb_[:, :])
                                    p.cp(pb_[:, :], pC[:, 0:128])
                                    p.trans(pT0[:, 0:128], pb_[:, :], idb[:, :])
                                    p.cp(ptb[:, :], pT0[:, 0:128])
                            for dsub in range(2):
                                ps = pT0 if dsub == 0 else pT1
                                p.trans(ps[:, 0:128], ub[:, 256 + 128 * dsub:256 + 128 * dsub + 128],
                                        idb[:, :])
                                p.cp(wtb[:, 128 * dsub:128 * dsub + 128], ps[:, 0:128])
                            if g > 0:
                                p.cp(sbf[:, :], pD[:, :])
                                for dsub in range(2):
                                    p.mm(pC[:, 0:256], wtb[:, 128 * dsub:128 * dsub + 128],
                                         sbf[:, 256 * dsub:256 * dsub + 256],
                                         start=(dsub == 0), stop=(dsub == 1))
                                p.tt(uf[:, 0:256], uf[:, 0:256], pC[:, 0:256], ALU.subtract)
                            p.cp(u2b[:, :], uf[:, 0:256])
                            for dsub in range(2):
                                p.mm(pC[:, 0:128], kct[:, 2048 * dsub + gc:2048 * dsub + gc + 128],
                                     qct[:, 2048 * dsub + gc:2048 * dsub + gc + 128],
                                     start=(dsub == 0), stop=(dsub == 1))
                            p.tt(sq[:, 0:128], pC[:, 0:128], mui[:, :], ALU.mult)
                            p.cp(atb[:, :], sq[:, 0:128])
                            if g > 0:
                                for dsub in range(2):
                                    p.mm(pE[:, 0:256], qct[:, 2048 * dsub + gc:2048 * dsub + gc + 128],
                                         sbf[:, 256 * dsub:256 * dsub + 256],
                                         start=(dsub == 0), stop=False)
                                p.mm(pE[:, 0:256], atb[:, :], u2b[:, :], start=False, stop=True)
                            else:
                                p.mm(pE[:, 0:256], atb[:, :], u2b[:, :], start=True, stop=True)
                            for dsub in range(2):
                                p.mm(pD[:, 256 * dsub:256 * dsub + 256],
                                     ktk[:, 256 * g + 128 * dsub:256 * g + 128 * dsub + 128],
                                     u2b[:, :], start=(g == 0), stop=(g == 15))
                            p.cp(sqb[:, 0:256], pE[:, 0:256])
                            trow = 2048 * b + gc
                            p.dma(do_d[trow:trow + 128, 256 * h:256 * h + 256],
                                  sqb[:, 0:256], eng="gpsimd")
            with ExitStack() as s2:
                w1b = s2.enter_context(sbn("w1b", [128, ND * 2048], BF16))
                w1s = s2.enter_context(sbn("w1s", [32, 2048], BF16))
                w2b = s2.enter_context(sbn("w2b", [128, 16 * 16], BF16))
                b2b = s2.enter_context(sbn("b2b", [1, 16], BF16))
                b1s = s2.enter_context(sbn("b1s", [128, 16], F32))
                brs = s2.enter_context(sbn("brs", [128, 4 * 1024], BF16))
                stt_ = s2.enter_context(sbn("stt_", [128, 128], BF16))
                stT = s2.enter_context(sbn("stT", [32, 4096], BF16))
                hT = s2.enter_context(sbn("hT", [128, 16 * 512], BF16))
                lgt = s2.enter_context(sbn("lgt", [128, 16], F32))
                prb = s2.enter_context(sbn("prb", [128, 16], F32))
                tmpr = s2.enter_context(sbn("tmpr", [1, 16], F32))
                tpb = s2.enter_context(sbn("tpb", [128, 16], F32))
                ot = s2.enter_context(sbn("ot", [128, 1024], F32))
                oTt = s2.enter_context(sbn("oTt", [128, 1024], BF16))
                wob = s2.enter_context(sbn("wob", [128, ND * 1024], BF16))
                onw = s2.enter_context(sbn("onw", [128, 2], F32))
                for k in range(ND):
                    p.dma(f2[:, :], a_w1[128 * k:128 * k + 128, 0:1024])
                    p.cp(w1b[:, 2048 * k:2048 * k + 1024], f2[:, :])
                    p.dma(f2[:, :], a_w1[128 * k:128 * k + 128, 1024:2048])
                    p.cp(w1b[:, 2048 * k + 1024:2048 * k + 2048], f2[:, :])
                p.dma(f2[0:32, 0:1024], a_w1[1024:1056, 0:1024])
                p.cp(w1s[:, 0:1024], f2[0:32, 0:1024])
                p.dma(f2[0:32, 0:1024], a_w1[1024:1056, 1024:2048])
                p.cp(w1s[:, 1024:2048], f2[0:32, 0:1024])
                for k in range(16):
                    p.dma(f2[:, 0:16], a_w2[128 * k:128 * k + 128, :])
                    p.cp(w2b[:, 16 * k:16 * k + 16], f2[:, 0:16])
                p.dma(f2[0:1, 0:16], a_b2)
                p.cp(b2b[:, :], f2[0:1, 0:16])
                p.dma(onw[:, 0:1], a_onw[0:128, :])
                p.dma(onw[:, 1:2], a_onw[128:256, :])
                for k in range(ND):
                    p.dma(f2[:, :], a_wo[128 * k:128 * k + 128, :])
                    p.ts(f2[:, :], f2[:, :], onw[:, (k % 2):(k % 2) + 1], ALU.mult)
                    p.cp(wob[:, 1024 * k:1024 * k + 1024], f2[:, :])
                p.dma(f2[0:1, 0:4], a_glt)
                p.act(f2[0:1, 12:16], f2[0:1, 0:4], AF.Abs)
                p.act(f2[0:1, 16:20], f2[0:1, 12:16], AF.Exp, scale=-1.0)
                p.ts(f2[0:1, 16:20], f2[0:1, 16:20], 1.0, ALU.add)
                p.act(f2[0:1, 20:24], f2[0:1, 16:20], AF.Ln)
                p.ts(f2[0:1, 4:8], f2[0:1, 0:4], 0.0, ALU.max)
                p.tt(f2[0:1, 4:8], f2[0:1, 4:8], f2[0:1, 20:24], ALU.add)
                p.ts(f2[0:1, 4:8], f2[0:1, 4:8], 1e-4, ALU.add)
                p.recip(f2[0:1, 8:12], f2[0:1, 4:8])
                for hh in range(4):
                    for jj in range(4):
                        p.cp(tmpr[0:1, 4 * hh + jj:4 * hh + jj + 1], f2[0:1, 8 + hh:8 + hh + 1])
                bcast_row_f32(tmpr, tpb, 16)
                for k in range(16):
                    p.dma(b1s[:, k:k + 1], a_b1[128 * k:128 * k + 128, :])
                for t in range(NT):
                    tr = 128 * t
                    for i, src in enumerate((fs_d, fl_d, do_d, vb_d)):
                        p.dma(brs[:, 1024 * i:1024 * i + 1024], src[tr:tr + 128, :])
                    for i in range(4):
                        for hh in range(4):
                            seg = brs[:, 1024 * i + 256 * hh:1024 * i + 256 * hh + 256]
                            p.rsum(sml[:, 8:9], seg)
                            p.ts(sml[:, 8:9], sml[:, 8:9], 1.0 / 256.0, ALU.mult)
                            p.act(sq[:, 0:256], seg, AF.Square, accum=sml[:, 9:10])
                            p.ts(sml[:, 9:10], sml[:, 9:10], 1.0 / 256.0, ALU.mult)
                            p.act(sml[:, 11:12], sml[:, 8:9], AF.Square)
                            p.tt(sml[:, 9:10], sml[:, 9:10], sml[:, 11:12], ALU.subtract)
                            p.ts(sml[:, 9:10], sml[:, 9:10], 1e-6, ALU.max)
                            p.act(sml[:, 10:11], sml[:, 9:10], AF.Sqrt)
                            p.cp(stt_[:, 8 * i + hh:8 * i + hh + 1], sml[:, 8:9])
                            p.cp(stt_[:, 8 * i + 4 + hh:8 * i + 4 + hh + 1], sml[:, 10:11])
                    p.mset(stt_[:, 32:128], 0.0)
                    p.trans(pT0[:, 0:128], stt_[:, :], idb[:, :])
                    p.cp(stT[:, 128 * t:128 * t + 128], pT0[0:32, 0:128])
                for ch in range(8):
                    for m in range(16):
                        for k in range(ND):
                            p.mm(pA[:, 0:512], w1b[:, 2048 * k + 128 * m:2048 * k + 128 * m + 128],
                                 yt[:, 4096 * k + 512 * ch:4096 * k + 512 * ch + 512],
                                 start=(k == 0), stop=False)
                        p.mm(pA[:, 0:512], w1s[:, 128 * m:128 * m + 128],
                             stT[:, 512 * ch:512 * ch + 512], start=False, stop=True)
                        p.act(hT[:, 512 * m:512 * m + 512], pA[:, 0:512], AF.Gelu,
                              bias=b1s[:, m:m + 1], scale=1.0)
                    for tl in range(4):
                        t = 4 * ch + tl
                        for m in range(16):
                            p.mm(pB[:, 0:16], hT[:, 512 * m + 128 * tl:512 * m + 128 * tl + 128],
                                 w2b[:, 16 * m:16 * m + 16], start=(m == 0), stop=False)
                        p.mm(pB[:, 0:16], onb[0:1, 0:128], b2b[:, :], start=False, stop=True)
                        p.cp(lgt[:, :], pB[:, 0:16])
                        p.tt(lgt[:, :], lgt[:, :], tpb[:, :], ALU.mult)
                        for hh in range(4):
                            sl = lgt[:, 4 * hh:4 * hh + 4]
                            p.rmax(sml[:, 12:13], sl)
                            p.ts(sml[:, 12:13], sml[:, 12:13], -1.0, ALU.mult)
                            p.act(prb[:, 4 * hh:4 * hh + 4], sl, AF.Exp,
                                  bias=sml[:, 12:13], scale=1.0, accum=sml[:, 13:14])
                            p.recip(sml[:, 14:15], sml[:, 13:14])
                            p.ts(prb[:, 4 * hh:4 * hh + 4], prb[:, 4 * hh:4 * hh + 4],
                                 sml[:, 14:15], ALU.mult)
                        tr = 128 * t
                        for i, src in enumerate((fs_d, fl_d, do_d, vb_d)):
                            p.dma(brs[:, 1024 * i:1024 * i + 1024], src[tr:tr + 128, :])
                        for hh in range(4):
                            for jj in range(4):
                                seg = brs[:, 1024 * jj + 256 * hh:1024 * jj + 256 * hh + 256]
                                wc = prb[:, 4 * hh + jj:4 * hh + jj + 1]
                                if jj == 0:
                                    p.ts(ot[:, 256 * hh:256 * hh + 256], seg, wc, ALU.mult)
                                else:
                                    p.stt(ot[:, 256 * hh:256 * hh + 256], seg, wc,
                                          ot[:, 256 * hh:256 * hh + 256], ALU.mult, ALU.add)
                            p.act(sq[:, 0:256], ot[:, 256 * hh:256 * hh + 256], AF.Square,
                                  accum=sml[:, 15:16])
                            rsqrt_(sml[:, 16:17], sml[:, 15:16], 1e-5, 1.0 / 256.0)
                            p.ts(ot[:, 256 * hh:256 * hh + 256], ot[:, 256 * hh:256 * hh + 256],
                                 sml[:, 16:17], ALU.mult)
                        p.cp(sqb[:, :], ot[:, :])
                        for k in range(ND):
                            ps = pT0 if k % 2 == 0 else pT1
                            p.trans(ps[:, 0:128], sqb[:, 128 * k:128 * k + 128], idb[:, :])
                            p.cp(oTt[:, 128 * k:128 * k + 128], ps[:, 0:128])
                        for nn in range(2):
                            ps = pC if nn == 0 else pD
                            for k in range(ND):
                                p.mm(ps[:, 0:512], oTt[:, 128 * k:128 * k + 128],
                                     wob[:, 1024 * k + 512 * nn:1024 * k + 512 * nn + 512],
                                     start=(k == 0), stop=(k == ND - 1))
                        p.cp(f2[:, 0:512], pC[:, 0:512])
                        p.cp(f2[:, 512:1024], pD[:, 0:512])
                        resid_add(xsrc, xdst, t, f2[:, :])

    def sub_ffn(xsrc, xdst, idx):
        with ExitStack() as s1:
            wbf = s1.enter_context(sbn("wbf8", [128, ND * 1024], BF16))
            wi8 = s1.enter_context(sbn("wi8", [128, 1024], I8))
            scb = s1.enter_context(sbn("scb", [128, 1024], F32))
            bcb = s1.enter_context(sbn("bcb", [128, 1024], F32))
            rmx = s1.enter_context(sbn("rmx", [128, 1], F32))
            it_ = s1.enter_context(sbn("it_", [128, 1024], I32))
            for k in range(ND):
                p.dma(wi8[:, :], f_w8[128 * k:128 * k + 128, :])
                p.cp(wbf[:, 1024 * k:1024 * k + 1024], wi8[:, :])
            for t in range(NT):
                p.dma(f1[:, :], xsrc[128 * t:128 * t + 128, :])
                ln_tile(f1[:, :], f2[:, :])
                p.dma(y_d[128 * t:128 * t + 128, :], f2[:, :], eng="gpsimd")
                p.rmax(sml[:, 20:21], f2[:, :], absv=True)
                if t == 0:
                    p.cp(rmx[:, :], sml[:, 20:21])
                else:
                    p.tt(rmx[:, :], rmx[:, :], sml[:, 20:21], ALU.max)
            p.cp(sqb[:, 0:1], rmx[:, :])
            p.trans(pT0[0:1, 0:128], sqb[:, 0:1], idb[:, :])
            p.cp(sq[0:1, 0:128], pT0[0:1, 0:128])
            p.rmax(sml[0:1, 21:22], sq[0:1, 0:128])
            p.dma(cc_i[idx], sml[0:1, 21:22], eng="gpsimd")
            p.coll_max(cc_i[idx], cc_o[idx])
            p.dma(sml[0:1, 22:23], cc_o[idx])
            p.ts(sml[0:1, 22:23], sml[0:1, 22:23], 1.0 / 127.0, ALU.mult)
            p.ts(sml[0:1, 22:23], sml[0:1, 22:23], 1e-5, ALU.max)
            p.recip(sml[0:1, 23:24], sml[0:1, 22:23])
            p.mm(pA[:, 0:1], onf[0:1, 0:128], sml[0:1, 23:24])
            p.cp(rmx[:, :], pA[:, 0:1])
            p.dma(f2[0:1, 0:1024], f_ws)
            p.ts(f2[0:1, 0:1024], f2[0:1, 0:1024], sml[0:1, 22:23], ALU.mult)
            bcast_row_f32(f2[0:1, 0:1024], scb, 1024)
            p.dma(f2[0:1, 0:1024], f_bi)
            bcast_row_f32(f2[0:1, 0:1024], bcb, 1024)
            for t in range(NT):
                p.dma(f2[:, :], y_d[128 * t:128 * t + 128, :])
                p.ts(f2[:, :], f2[:, :], rmx[:, 0:1], ALU.mult)
                p.cp(it_[:, :], f2[:, :])
                p.cp(f2[:, :], it_[:, :])
                p.ts(f2[:, :], f2[:, :], 127.0, ALU.min)
                p.ts(f2[:, :], f2[:, :], -128.0, ALU.max)
                to_yt(f2[:, :], t)
            for t in range(NT):
                for nn in range(2):
                    ps = pC if nn == 0 else pD
                    for k in range(ND):
                        p.mm(ps[:, 0:512], yt[:, 4096 * k + 128 * t:4096 * k + 128 * t + 128],
                             wbf[:, 1024 * k + 512 * nn:1024 * k + 512 * nn + 512],
                             start=(k == 0), stop=(k == ND - 1))
                p.tt(f2[:, 0:512], pC[:, 0:512], scb[:, 0:512], ALU.mult)
                p.tt(f2[:, 512:1024], pD[:, 0:512], scb[:, 512:1024], ALU.mult)
                p.tt(f2[:, :], f2[:, :], bcb[:, :], ALU.add)
                resid_add(xsrc, xdst, t, f2[:, :])

    def attn_core(b, h, q_sb, k_sb, vall, oall, causal, pbs, ob64, vcol, vstride=1024):
        for c in range(4):
            cl = 512 * c
            for m in range(16):
                mb = 128 * m
                if causal and cl + 512 <= mb:
                    p.mset(pbs[:, 0:512], 1.0)
                else:
                    p.mm(pA[:, 0:512], k_sb[:, mb:mb + 128], q_sb[:, cl:cl + 512])
                    p.act(pbs[:, 0:512], pA[:, 0:512], AF.Exp, scale=0.125)
                    if causal and cl <= mb < cl + 512:
                        d0 = mb - cl
                        if d0 > 0:
                            p.mset(pbs[:, 0:d0], 1.0)
                        blk = pbs[:, d0:d0 + 128]
                        p.tt(blk, blk, muib[:, :], ALU.mult)
                        p.tt(blk, blk, mlsb[:, :], ALU.add)
                p.mm(pB[0:64, 0:512], vall[:, vstride * m + vcol:vstride * m + vcol + 64],
                     pbs[:, 0:512], start=(m == 0), stop=(m == 15))
                p.mm(pC[0:1, 0:512], onc[:, :], pbs[:, 0:512], start=(m == 0), stop=(m == 15))
            p.cp(sq[0:1, 0:512], pC[0:1, 0:512])
            p.recip(sq[0:1, 512:1024], sq[0:1, 0:512])
            p.mm(pD[0:64, 0:512], onf[0:1, 0:64], sq[0:1, 512:1024])
            r0 = 64 * (h % 2)
            p.cp(f1[0:64, 0:512], pD[0:64, 0:512])
            p.tt(ob64[0:64, 0:512], pB[0:64, 0:512], f1[0:64, 0:512], ALU.mult)
            p.dma(oall[r0:r0 + 64, 2048 * (h // 2) + cl:2048 * (h // 2) + cl + 512],
                  ob64[0:64, 0:512], eng="gpsimd")

    def sub_ab(xsrc, xdst):
        emit_ln_yt(xsrc)
        with ExitStack() as s1:
            wqt = s1.enter_context(sbn("wqt", [128, 1024], F32))
            iq = s1.enter_context(sbn("iq", [128, 1024], I32))
            for wsrc, wdst in ((b_wq, wq_s), (b_wk, wk_s), (b_wv, wv_s)):
                for r in range(ND):
                    p.dma(wqt[:, :], wsrc[128 * r:128 * r + 128, :])
                    p.rmax(sml[:, 24:25], wqt[:, :], absv=True)
                    p.ts(sml[:, 24:25], sml[:, 24:25], 1.0 / 127.0, ALU.mult)
                    p.ts(sml[:, 24:25], sml[:, 24:25], 1e-8, ALU.max)
                    p.recip(sml[:, 25:26], sml[:, 24:25])
                    p.ts(wqt[:, :], wqt[:, :], sml[:, 25:26], ALU.mult)
                    p.cp(iq[:, :], wqt[:, :])
                    p.cp(wqt[:, :], iq[:, :])
                    p.ts(wqt[:, :], wqt[:, :], 127.0, ALU.min)
                    p.ts(wqt[:, :], wqt[:, :], -127.0, ALU.max)
                    p.ts(wqt[:, :], wqt[:, :], sml[:, 24:25], ALU.mult)
                    p.cp(sqb[:, :], wqt[:, :])
                    for k in range(ND):
                        ps = pT0 if k % 2 == 0 else pT1
                        p.trans(ps[:, 0:128], sqb[:, 128 * k:128 * k + 128], idb[:, :])
                        p.cp(f2[:, 0:128], ps[:, 0:128])
                        p.cp(sqb[:, 0:0] if False else f1[:, 0:128], f2[:, 0:128]) if False else None
                        p.cp(it_dummy := None, None) if False else None
                        p.dma(wdst[128 * k:128 * k + 128, 128 * r:128 * r + 128],
                              f2[:, 0:128], eng="gpsimd")
        with ExitStack() as s1:
            bqb = s1.enter_context(sbn("bqb", [1, 1024], BF16))
            bkb = s1.enter_context(sbn("bkb", [1, 1024], BF16))
            bvb = s1.enter_context(sbn("bvb", [1, 1024], BF16))
            p.dma(f2[0:1, :], b_bq)
            p.cp(bqb[:, :], f2[0:1, :])
            p.dma(f2[0:1, :], b_bk)
            p.cp(bkb[:, :], f2[0:1, :])
            p.dma(f2[0:1, :], b_bv)
            p.cp(bvb[:, :], f2[0:1, :])
            for b in range(B):
                with ExitStack() as s2:
                    vall = s2.enter_context(sbn("vall", [128, 16 * 1024], BF16))
                    oall = s2.enter_context(sbn("oall", [128, 8 * 2048], BF16))
                    q_sb = s2.enter_context(sbn("q_sb", [64, 2048], BF16))
                    k_sb = s2.enter_context(sbn("k_sb", [64, 2048], BF16))
                    hw = s2.enter_context(sbn("hw", [128, 1024], BF16))
                    pbs = s2.enter_context(sbn("pbs", [128, 512], BF16))
                    with sbn("wvb", [128, ND * 1024], BF16) as wvb:
                        for k in range(ND):
                            p.dma(wvb[:, 1024 * k:1024 * k + 1024], wv_s[128 * k:128 * k + 128, :])
                        for t_ in range(16):
                            for nn in range(2):
                                ps = pC if nn == 0 else pD
                                p.mm(ps[:, 0:512], onb[0:1, 0:128],
                                     bvb[0:1, 512 * nn:512 * nn + 512], start=True, stop=False)
                                for k in range(ND):
                                    p.mm(ps[:, 0:512],
                                         yt[:, 4096 * k + 2048 * b + 128 * t_:
                                            4096 * k + 2048 * b + 128 * t_ + 128],
                                         wvb[:, 1024 * k + 512 * nn:1024 * k + 512 * nn + 512],
                                         start=False, stop=(k == ND - 1))
                                p.cp(vall[:, 1024 * t_ + 512 * nn:1024 * t_ + 512 * nn + 512],
                                     ps[:, 0:512])
                    for h in range(16):
                        for k in range(ND):
                            p.dma(hw[:, 64 * k:64 * k + 64],
                                  wq_s[128 * k:128 * k + 128, 64 * h:64 * h + 64])
                            p.dma(hw[:, 512 + 64 * k:512 + 64 * k + 64],
                                  wk_s[128 * k:128 * k + 128, 64 * h:64 * h + 64])
                        for c in range(4):
                            p.mm(pA[0:64, 0:512], bqb[0:1, 64 * h:64 * h + 64], onb[0:1, 0:512],
                                 start=True, stop=False)
                            for k in range(ND):
                                p.mm(pA[0:64, 0:512], hw[:, 64 * k:64 * k + 64],
                                     yt[:, 4096 * k + 2048 * b + 512 * c:
                                        4096 * k + 2048 * b + 512 * c + 512],
                                     start=False, stop=(k == ND - 1))
                            p.cp(q_sb[:, 512 * c:512 * c + 512], pA[0:64, 0:512])
                            p.mm(pB[0:64, 0:512], bkb[0:1, 64 * h:64 * h + 64], onb[0:1, 0:512],
                                 start=True, stop=False)
                            for k in range(ND):
                                p.mm(pB[0:64, 0:512], hw[:, 512 + 64 * k:512 + 64 * k + 64],
                                     yt[:, 4096 * k + 2048 * b + 512 * c:
                                        4096 * k + 2048 * b + 512 * c + 512],
                                     start=False, stop=(k == ND - 1))
                            p.cp(k_sb[:, 512 * c:512 * c + 512], pB[0:64, 0:512])
                        attn_core(b, h, q_sb, k_sb, vall, oall, False, pbs, ob64, 64 * h)
                    for t_ in range(16):
                        for k in range(ND):
                            ps = pT0 if k % 2 == 0 else pT1
                            p.trans(ps[:, 0:128], oall[:, 2048 * k + 128 * t_:2048 * k + 128 * t_ + 128],
                                    idb[:, :])
                            p.cp(f2[:, 128 * k:128 * k + 128], ps[:, 0:128])
                        resid_add(xsrc, xdst, 16 * b + t_, f2[:, :])

    def sub_l1aa(xsrc, xdst):
        emit_ln_yt(xsrc)
        with ExitStack() as s1:
            wqa = s1.enter_context(sbn("wqa", [128, ND * 1024], BF16))
            wkv = s1.enter_context(sbn("wkv", [128, ND * 128], BF16))
            wop = s1.enter_context(sbn("wop", [128, ND * 1024], BF16))
            for k in range(ND):
                p.dma(f2[:, :], c_qp[128 * k:128 * k + 128, :])
                p.cp(wqa[:, 1024 * k:1024 * k + 1024], f2[:, :])
                p.dma(f2[:, 0:64], c_kp[128 * k:128 * k + 128, :])
                p.cp(wkv[:, 128 * k:128 * k + 64], f2[:, 0:64])
                p.dma(f2[:, 0:64], c_vp[128 * k:128 * k + 128, :])
                p.cp(wkv[:, 128 * k + 64:128 * k + 128], f2[:, 0:64])
                p.dma(f2[:, :], c_op[128 * k:128 * k + 128, :])
                p.cp(wop[:, 1024 * k:1024 * k + 1024], f2[:, :])
            for b in range(B):
                with ExitStack() as s2:
                    vall = s2.enter_context(sbn("vall", [128, 16 * 64], BF16))
                    oall = s2.enter_context(sbn("oall", [128, 8 * 2048], BF16))
                    q_sb = s2.enter_context(sbn("q_sb", [64, 2048], BF16))
                    k_sb = s2.enter_context(sbn("k_sb", [64, 2048], BF16))
                    pbs = s2.enter_context(sbn("pbs", [128, 512], BF16))
                    for c in range(4):
                        for k in range(ND):
                            p.mm(pB[0:64, 0:512], wkv[:, 128 * k:128 * k + 64],
                                 yt[:, 4096 * k + 2048 * b + 512 * c:4096 * k + 2048 * b + 512 * c + 512],
                                 start=(k == 0), stop=(k == ND - 1))
                        p.cp(k_sb[:, 512 * c:512 * c + 512], pB[0:64, 0:512])
                    for t_ in range(16):
                        for k in range(ND):
                            p.mm(pC[:, 0:64],
                                 yt[:, 4096 * k + 2048 * b + 128 * t_:4096 * k + 2048 * b + 128 * t_ + 128],
                                 wkv[:, 128 * k + 64:128 * k + 128],
                                 start=(k == 0), stop=(k == ND - 1))
                        p.cp(vall[:, 64 * t_:64 * t_ + 64], pC[:, 0:64])
                    for h in range(16):
                        for c in range(4):
                            for k in range(ND):
                                p.mm(pA[0:64, 0:512], wqa[:, 1024 * k + 64 * h:1024 * k + 64 * h + 64],
                                     yt[:, 4096 * k + 2048 * b + 512 * c:
                                        4096 * k + 2048 * b + 512 * c + 512],
                                     start=(k == 0), stop=(k == ND - 1))
                            p.cp(q_sb[:, 512 * c:512 * c + 512], pA[0:64, 0:512])
                        attn_core(b, h, q_sb, k_sb, vall, oall, False, pbs, ob64, 0, vstride=64)
                    for t_ in range(16):
                        for nn in range(2):
                            ps = pC if nn == 0 else pD
                            for k in range(ND):
                                p.mm(ps[:, 0:512],
                                     oall[:, 2048 * k + 128 * t_:2048 * k + 128 * t_ + 128],
                                     wop[:, 1024 * k + 512 * nn:1024 * k + 512 * nn + 512],
                                     start=(k == 0), stop=(k == ND - 1))
                        p.cp(f2[:, 0:512], pC[:, 0:512])
                        p.cp(f2[:, 512:1024], pD[:, 0:512])
                        resid_add(xsrc, xdst, 16 * b + t_, f2[:, :])

    def sub_l1ab(xsrc, xdst):
        with ExitStack() as s1:
            l1wb = s1.enter_context(sbn("l1wb", [128, 1024], F32))
            l1bb = s1.enter_context(sbn("l1bb", [128, 1024], F32))
            l2wb = s1.enter_context(sbn("l2wb", [128, 1024], F32))
            l2bb = s1.enter_context(sbn("l2bb", [128, 1024], F32))
            bqb = s1.enter_context(sbn("bqb2", [1, 1024], BF16))
            bkb = s1.enter_context(sbn("bkb2", [1, 1024], BF16))
            bvb = s1.enter_context(sbn("bvb2", [1, 1024], BF16))
            bob = s1.enter_context(sbn("bob2", [1, 1024], BF16))
            p.dma(f2[0:1, :], d_l1w)
            bcast_row_f32(f2[0:1, :], l1wb, 1024)
            p.dma(f2[0:1, :], d_l1b)
            bcast_row_f32(f2[0:1, :], l1bb, 1024)
            p.dma(f2[0:1, :], d_l2w)
            bcast_row_f32(f2[0:1, :], l2wb, 1024)
            p.dma(f2[0:1, :], d_l2b)
            bcast_row_f32(f2[0:1, :], l2bb, 1024)
            for src, dstb in ((d_bq, bqb), (d_bk, bkb), (d_bv, bvb), (d_bo, bob)):
                p.dma(f2[0:1, :], src)
                p.cp(dstb[:, :], f2[0:1, :])
            emit_ln_yt(xsrc, store_y=True, double_ln=(l1wb[:, :], l1bb[:, :]))
            for b in range(B):
                with ExitStack() as s2:
                    oall = s2.enter_context(sbn("oall", [128, 8 * 2048], BF16))
                    s3 = s2.enter_context(ExitStack())
                    vall = s3.enter_context(sbn("vall", [128, 16 * 1024], BF16))
                    q_sb = s3.enter_context(sbn("q_sb", [64, 2048], BF16))
                    k_sb = s3.enter_context(sbn("k_sb", [64, 2048], BF16))
                    hw = s3.enter_context(sbn("hw", [128, 1024], BF16))
                    pbs = s3.enter_context(sbn("pbs", [128, 512], BF16))
                    wvb = s3.enter_context(sbn("wvb2", [128, 1024], BF16))
                    for h in range(16):
                        for k in range(ND):
                            p.dma(f2[:, 0:64], d_wv[128 * k:128 * k + 128, 64 * h:64 * h + 64])
                            p.cp(wvb[:, 64 * k:64 * k + 64], f2[:, 0:64])
                        for t_ in range(16):
                            p.mm(pC[:, 0:64], onb[0:1, 0:128], bvb[0:1, 64 * h:64 * h + 64],
                                 start=True, stop=False)
                            for k in range(ND):
                                p.mm(pC[:, 0:64],
                                     yt[:, 4096 * k + 2048 * b + 128 * t_:
                                        4096 * k + 2048 * b + 128 * t_ + 128],
                                     wvb[:, 64 * k:64 * k + 64], start=False, stop=(k == ND - 1))
                            p.cp(vall[:, 1024 * t_ + 64 * h:1024 * t_ + 64 * h + 64], pC[:, 0:64])
                    for h in range(16):
                        for k in range(ND):
                            p.dma(f2[:, 0:64], d_wq[128 * k:128 * k + 128, 64 * h:64 * h + 64])
                            p.cp(hw[:, 64 * k:64 * k + 64], f2[:, 0:64])
                            p.dma(f2[:, 0:64], d_wk[128 * k:128 * k + 128, 64 * h:64 * h + 64])
                            p.cp(hw[:, 512 + 64 * k:512 + 64 * k + 64], f2[:, 0:64])
                        for c in range(4):
                            p.mm(pA[0:64, 0:512], bqb[0:1, 64 * h:64 * h + 64], onb[0:1, 0:512],
                                 start=True, stop=False)
                            for k in range(ND):
                                p.mm(pA[0:64, 0:512], hw[:, 64 * k:64 * k + 64],
                                     yt[:, 4096 * k + 2048 * b + 512 * c:
                                        4096 * k + 2048 * b + 512 * c + 512],
                                     start=False, stop=(k == ND - 1))
                            p.cp(q_sb[:, 512 * c:512 * c + 512], pA[0:64, 0:512])
                            p.mm(pB[0:64, 0:512], bkb[0:1, 64 * h:64 * h + 64], onb[0:1, 0:512],
                                 start=True, stop=False)
                            for k in range(ND):
                                p.mm(pB[0:64, 0:512], hw[:, 512 + 64 * k:512 + 64 * k + 64],
                                     yt[:, 4096 * k + 2048 * b + 512 * c:
                                        4096 * k + 2048 * b + 512 * c + 512],
                                     start=False, stop=(k == ND - 1))
                            p.cp(k_sb[:, 512 * c:512 * c + 512], pB[0:64, 0:512])
                        attn_core(b, h, q_sb, k_sb, vall, oall, True, pbs, ob64, 64 * h)
                    s3.close()
                    with sbn("wo2", [128, ND * 1024], BF16) as wo2:
                        lnt = comb
                        for k in range(ND):
                            p.dma(f2[:, :], d_wo[128 * k:128 * k + 128, :])
                            p.cp(wo2[:, 1024 * k:1024 * k + 1024], f2[:, :])
                        for t_ in range(16):
                            for nn in range(2):
                                ps = pC if nn == 0 else pD
                                p.mm(ps[:, 0:512], onb[0:1, 0:128],
                                     bob[0:1, 512 * nn:512 * nn + 512], start=True, stop=False)
                                for k in range(ND):
                                    p.mm(ps[:, 0:512],
                                         oall[:, 2048 * k + 128 * t_:2048 * k + 128 * t_ + 128],
                                         wo2[:, 1024 * k + 512 * nn:1024 * k + 512 * nn + 512],
                                         start=False, stop=(k == ND - 1))
                            p.cp(f2[:, 0:512], pC[:, 0:512])
                            p.cp(f2[:, 512:1024], pD[:, 0:512])
                            t = 16 * b + t_
                            p.dma(f1[:, :], y_d[128 * t:128 * t + 128, :])
                            p.tt(f2[:, :], f2[:, :], f1[:, :], ALU.add)
                            p.cp(f1[:, :], f2[:, :])
                            ln_tile(f1[:, :], lnt[:, :])
                            p.tt(lnt[:, :], lnt[:, :], l2wb[:, :], ALU.mult)
                            p.tt(lnt[:, :], lnt[:, :], l2bb[:, :], ALU.add)
                            p.tt(f2[:, :], f2[:, :], lnt[:, :], ALU.add)
                            resid_add(xsrc, xdst, t, f2[:, :])

    def sub_moe(xsrc, xdst):
        with ExitStack() as s1:
            gwb = s1.enter_context(sbn("gwb", [128, ND * 32], BF16))
            gwl = s1.enter_context(sbn("gwl", [128, ND * 32], BF16))
            cbb = s1.enter_context(sbn("cbb", [128, 32], F32))
            sc_ = s1.enter_context(sbn("sc_", [128, 32], F32))
            sfc = s1.enter_context(sbn("sfc", [128, 32], F32))
            gs8 = s1.enter_context(sbn("gs8", [128, 8], F32))
            rk8 = s1.enter_context(sbn("rk8", [128, 8], F32))
            t8 = s1.enter_context(sbn("t8", [128, 8], F32))
            msk = s1.enter_context(sbn("msk", [128, 32], F32))
            rk32 = s1.enter_context(sbn("rk32", [128, 32], F32))
            t32 = s1.enter_context(sbn("t32", [128, 32], F32))
            ylb = s1.enter_context(sbn("ylb", [128, 1024], BF16))
            ylT = s1.enter_context(sbn("ylT", [128, 1024], BF16))
            for k in range(ND):
                p.dma(f2[:, 0:32], m_gw[128 * k:128 * k + 128, :])
                p.cp(gwb[:, 32 * k:32 * k + 32], f2[:, 0:32])
                p.cp(sq[:, 0:32], gwb[:, 32 * k:32 * k + 32])
                p.tt(sq[:, 32:64], f2[:, 0:32], sq[:, 0:32], ALU.subtract)
                p.cp(gwl[:, 32 * k:32 * k + 32], sq[:, 32:64])
            p.dma(f2[0:1, 0:32], m_cb)
            bcast_row_f32(f2[0:1, 0:32], cbb, 32)
            for t in range(NT):
                p.dma(f1[:, :], xsrc[128 * t:128 * t + 128, :])
                ln_tile(f1[:, :], f2[:, :])
                to_yt(f2[:, :], t)
                p.cp(f1[:, :], sqb[:, :])
                p.tt(f1[:, :], f2[:, :], f1[:, :], ALU.subtract)
                p.cp(ylb[:, :], f1[:, :])
                for k in range(ND):
                    ps = pT0 if k % 2 == 0 else pT1
                    p.trans(ps[:, 0:128], ylb[:, 128 * k:128 * k + 128], idb[:, :])
                    p.cp(ylT[:, 128 * k:128 * k + 128], ps[:, 0:128])
                for k in range(ND):
                    ysl = yt[:, 4096 * k + 128 * t:4096 * k + 128 * t + 128]
                    p.mm(pC[:, 0:32], ysl, gwb[:, 32 * k:32 * k + 32],
                         start=(k == 0), stop=False)
                    p.mm(pC[:, 0:32], ylT[:, 128 * k:128 * k + 128],
                         gwb[:, 32 * k:32 * k + 32], start=False, stop=False)
                    p.mm(pC[:, 0:32], ysl, gwl[:, 32 * k:32 * k + 32],
                         start=False, stop=(k == ND - 1))
                p.act(sc_[:, :], pC[:, 0:32], AF.Sigmoid)
                p.tt(sfc[:, :], sc_[:, :], cbb[:, :], ALU.add)
                for g in range(8):
                    p.rmax(gs8[:, g:g + 1], sfc[:, 4 * g:4 * g + 4])
                for j in range(8):
                    p.ts(t8[:, :], gs8[:, :], gs8[:, j:j + 1], ALU.is_gt)
                    if j == 0:
                        p.cp(rk8[:, :], t8[:, :])
                    else:
                        p.tt(rk8[:, :], rk8[:, :], t8[:, :], ALU.add)
                p.ts(rk8[:, :], rk8[:, :], 3.5, ALU.is_ge)
                for g in range(8):
                    p.ts(msk[:, 4 * g:4 * g + 4], sfc[:, 4 * g:4 * g + 4],
                         rk8[:, g:g + 1], ALU.mult)
                for j in range(32):
                    p.ts(t32[:, :], msk[:, :], msk[:, j:j + 1], ALU.is_gt)
                    if j == 0:
                        p.cp(rk32[:, :], t32[:, :])
                    else:
                        p.tt(rk32[:, :], rk32[:, :], t32[:, :], ALU.add)
                p.ts(rk32[:, :], rk32[:, :], 25.5, ALU.is_ge)
                p.tt(t32[:, :], rk32[:, :], sc_[:, :], ALU.mult)
                p.rsum(sml[:, 28:29], t32[:, :])
                p.ts(sml[:, 28:29], sml[:, 28:29], 1e-20, ALU.add)
                p.recip(sml[:, 29:30], sml[:, 28:29])
                p.ts(sml[:, 29:30], sml[:, 29:30], 2.5, ALU.mult)
                p.ts(comb[:, 32 * t:32 * t + 32], t32[:, :], sml[:, 29:30], ALU.mult)
        with ExitStack() as s1:
            accm = s1.enter_context(sbn("accm", [128, 16 * 1024], F32))
            wgu = s1.enter_context(sbn("wgu", [128, ND * 1024], BF16))
            wdn = s1.enter_context(sbn("wdn", [128, 4 * 1024], BF16))
            hbf = s1.enter_context(sbn("hbf", [128, 4 * 2048], BF16))
            for cchunk in range(2):
                tok0 = 2048 * cchunk
                for e in range(34):
                    for k in range(ND):
                        if e < 32:
                            p.dma(f2[:, :], m_gu[1024 * e + 128 * k:1024 * e + 128 * k + 128, :])
                        else:
                            c0 = 512 * (e - 32)
                            p.dma(f2[:, 0:512], m_sg[128 * k:128 * k + 128, c0:c0 + 512])
                            p.dma(f2[:, 512:1024], m_su[128 * k:128 * k + 128, c0:c0 + 512])
                        p.cp(wgu[:, 1024 * k:1024 * k + 1024], f2[:, :])
                    for it in range(4):
                        if e < 32:
                            p.dma(f2[:, :], m_dn[512 * e + 128 * it:512 * e + 128 * it + 128, :])
                        else:
                            r0 = 512 * (e - 32) + 128 * it
                            p.dma(f2[:, :], m_sd[r0:r0 + 128, :])
                        p.cp(wdn[:, 1024 * it:1024 * it + 1024], f2[:, :])
                    for nch in range(4):
                        tc = tok0 + 512 * nch
                        for it in range(4):
                            for k in range(ND):
                                p.mm(pA[:, 0:512], wgu[:, 1024 * k + 128 * it:1024 * k + 128 * it + 128],
                                     yt[:, 4096 * k + tc:4096 * k + tc + 512],
                                     start=(k == 0), stop=(k == ND - 1))
                                p.mm(pB[:, 0:512],
                                     wgu[:, 1024 * k + 512 + 128 * it:1024 * k + 512 + 128 * it + 128],
                                     yt[:, 4096 * k + tc:4096 * k + tc + 512],
                                     start=(k == 0), stop=(k == ND - 1))
                            p.act(sq[:, 0:512], pA[:, 0:512], AF.Silu)
                            p.tt(hbf[:, 2048 * it + 512 * nch:2048 * it + 512 * nch + 512],
                                 sq[:, 0:512], pB[:, 0:512], ALU.mult)
                    for mt in range(16):
                        tg = 16 * cchunk + mt
                        for nn in range(2):
                            ps = pC if nn == 0 else pD
                            for it in range(4):
                                p.mm(ps[:, 0:512],
                                     hbf[:, 2048 * it + 128 * mt:2048 * it + 128 * mt + 128],
                                     wdn[:, 1024 * it + 512 * nn:1024 * it + 512 * nn + 512],
                                     start=(it == 0), stop=(it == 3))
                        if e < 32:
                            ccol = comb[:, 32 * tg + e:32 * tg + e + 1]
                            p.ts(f2[:, 0:512], pC[:, 0:512], ccol, ALU.mult)
                            p.ts(f2[:, 512:1024], pD[:, 0:512], ccol, ALU.mult)
                        else:
                            p.cp(f2[:, 0:512], pC[:, 0:512])
                            p.cp(f2[:, 512:1024], pD[:, 0:512])
                        if e == 0:
                            p.cp(accm[:, 1024 * mt:1024 * mt + 1024], f2[:, :])
                        else:
                            p.tt(accm[:, 1024 * mt:1024 * mt + 1024],
                                 accm[:, 1024 * mt:1024 * mt + 1024], f2[:, :], ALU.add)
                for mt in range(16):
                    resid_add(xsrc, xdst, 16 * cchunk + mt, accm[:, 1024 * mt:1024 * mt + 1024])

    import os
    NSUB = int(os.environ.get("NSUB", "8"))
    subs = [lambda d: sub_aa(hs, d), lambda d: sub_ffn(x_d, d, 0),
            lambda d: sub_ab(x_d, d), lambda d: sub_ffn(x_d, d, 1),
            lambda d: sub_l1aa(x_d, d), lambda d: sub_moe(x_d, d),
            lambda d: sub_l1ab(x_d, d), lambda d: sub_moe(x_d, d)]
    for i in range(NSUB):
        subs[i](out_d if i == NSUB - 1 else x_d)

    es.close()

    g = nc.alloc_semaphore(name="g")
    total = p.cum
    with nc.Block() as blk:
        def mk(name):
            def body(eng):
                for (w, fn, inc) in p.q[name]:
                    if w > 0:
                        eng.wait_ge(g, w)
                    fn(eng).then_inc(g, inc)
                if name == "gpsimd":
                    eng.wait_ge(g, total)
            return body
        blk.sync(mk("sync"))
        blk.scalar(mk("scalar"))
        blk.vector(mk("vector"))
        blk.tensor(mk("tensor"))
        blk.gpsimd(mk("gpsimd"))
    return nc


def kernel(**inp):
    f = np.ascontiguousarray
    x = inp["L0_attn_a_hidden_states"]
    eye = np.eye(128, dtype=np.float32)
    ones = np.ones((1, 512), dtype=np.float32)
    r = np.arange(128)
    mls_ = (r[:, None] > r[None, :]).astype(np.float32)
    mnus_ = -(r[:, None] < r[None, :]).astype(np.float32)
    mui_ = (r[:, None] <= r[None, :]).astype(np.float32)

    common = {
        "a_wq": f(inp["L0_attn_a_Wq"]), "a_wk": f(inp["L0_attn_a_Wk"]),
        "a_wv": f(inp["L0_attn_a_Wv"]), "a_wb": f(inp["L0_attn_a_Wb"]),
        "a_cq": f(inp["L0_attn_a_conv_q_w"]), "a_ck": f(inp["L0_attn_a_conv_k_w"]),
        "a_cv": f(inp["L0_attn_a_conv_v_w"]),
        "a_fs": f(inp["L0_attn_a_fir_short_w"].reshape(1024, 3)),
        "a_fl": f(inp["L0_attn_a_fir_long_w"].reshape(1024, 63)),
        "a_w1": f(inp["L0_attn_a_mlp_w1"]),
        "a_b1": f(inp["L0_attn_a_mlp_b1"].reshape(2048, 1)),
        "a_w2": f(inp["L0_attn_a_mlp_w2"]),
        "a_b2": f(inp["L0_attn_a_mlp_b2"].reshape(1, 16)),
        "a_glt": f(inp["L0_attn_a_gate_log_temp"].reshape(1, 4)),
        "a_onw": f(inp["L0_attn_a_o_norm_w"].reshape(256, 1)),
        "a_wo": f(inp["L0_attn_a_Wo"]),
        "b_wq": f(inp["L0_attn_b_Wq"]), "b_wk": f(inp["L0_attn_b_Wk"]),
        "b_wv": f(inp["L0_attn_b_Wv"]),
        "b_bq": f(inp["L0_attn_b_bq"].reshape(1, 1024)),
        "b_bk": f(inp["L0_attn_b_bk"].reshape(1, 1024)),
        "b_bv": f(inp["L0_attn_b_bv"].reshape(1, 1024)),
        "f_w8": f(inp["L0_ffn_weight_int8"].T),
        "f_ws": f(inp["L0_ffn_weight_scale"].reshape(1, 1024)),
        "f_bi": f(inp["L0_ffn_bias"].reshape(1, 1024)),
        "c_qp": f(inp["L1_attn_a_query_proj"].transpose(2, 0, 1).reshape(1024, 1024)),
        "c_kp": f(inp["L1_attn_a_key_proj"]),
        "c_vp": f(inp["L1_attn_a_value_proj"]),
        "c_op": f(inp["L1_attn_a_output_proj"].transpose(1, 2, 0).reshape(1024, 1024)),
        "d_wq": f(inp["L1_attn_b_W_Q"].transpose(1, 0, 2).reshape(1024, 1024)),
        "d_wk": f(inp["L1_attn_b_W_K"].transpose(1, 0, 2).reshape(1024, 1024)),
        "d_wv": f(inp["L1_attn_b_W_V"].transpose(1, 0, 2).reshape(1024, 1024)),
        "d_wo": f(inp["L1_attn_b_W_O"].reshape(1024, 1024)),
        "d_bq": f(inp["L1_attn_b_b_Q"].reshape(1, 1024)),
        "d_bk": f(inp["L1_attn_b_b_K"].reshape(1, 1024)),
        "d_bv": f(inp["L1_attn_b_b_V"].reshape(1, 1024)),
        "d_bo": f(inp["L1_attn_b_b_O"].reshape(1, 1024)),
        "d_l1w": f(inp["L1_attn_b_ln1_w"].reshape(1, 1024)),
        "d_l1b": f(inp["L1_attn_b_ln1_b"].reshape(1, 1024)),
        "d_l2w": f(inp["L1_attn_b_ln2_w"].reshape(1, 1024)),
        "d_l2b": f(inp["L1_attn_b_ln2_b"].reshape(1, 1024)),
        "m_gw": f(inp["L1_ffn_gate_weight"].T),
        "m_cb": f(inp["L1_ffn_e_score_correction_bias"].reshape(1, 32)),
        "m_gu": f(inp["L1_ffn_experts_gate_up"].transpose(0, 2, 1).reshape(32 * 1024, 1024)),
        "m_dn": f(inp["L1_ffn_experts_down"].transpose(0, 2, 1).reshape(32 * 512, 1024)),
        "m_sg": f(inp["L1_ffn_shared_gate_w"].T),
        "m_su": f(inp["L1_ffn_shared_up_w"].T),
        "m_sd": f(inp["L1_ffn_shared_down_w"].T),
        "c_eye": eye, "c_ones": ones,
        "c_mls": mls_, "c_mnus": mnus_, "c_mui": mui_,
    }
    in_maps = []
    for c in range(8):
        m = dict(common)
        m["hs"] = f(x[2 * c:2 * c + 2].reshape(T, D))
        in_maps.append(m)
    nc = build()
    res = run_bass_kernel_spmd(nc, in_maps, core_ids=list(range(8)))
    out = np.concatenate([res.results[c]["out"].reshape(2, L, D) for c in range(8)], axis=0)
    return out.astype(np.float32)
